# version 12
# baseline (speedup 1.0000x reference)
import numpy as np
import concourse.bass as bass
import concourse.bacc as bacc
from concourse.bass import ts, ds
from concourse.expressions_rust import s_assert_within
import concourse.mybir as mybir
from concourse.bass_utils import run_bass_kernel_spmd

F32 = mybir.dt.float32
BF16 = mybir.dt.bfloat16
I32 = mybir.dt.int32
ALU = mybir.AluOpType
AF = mybir.ActivationFunctionType
AX = mybir.AxisListType

NCORES = 8
LN_EPS = 1e-5
NENG = 5


class Cfg:
    def __init__(self, D=4096, U=1024, F=9984, FGS=384, T=2048, SUP=2, G=6):
        self.D, self.U, self.F, self.FGS, self.T, self.G = D, U, F, FGS, T, G
        self.NCL, self.NCH = 16, 256
        self.NT = T // 128
        self.SUP = SUP
        self.NSUP = self.NT // SUP
        self.ICH = D // 128
        self.OG = D // 512
        self.UCH = U // 128
        self.UG = U // 512
        self.NFG = F // FGS
        self.FCH = F // 128
        self.FPG = FGS // 128
        self.OSH = D // NCORES
        self.OT = self.OSH // 128
        self.QW = D // 32
        self.RPOG = 512 // self.OSH if self.OSH < 512 else 1
        self.THB = float(2 * np.spacing(np.float32(D)))
        assert D % 512 == 0 and U % 512 == 0 and F % FGS == 0 and FGS % 128 == 0
        assert T % 128 == 0 and self.NT % SUP == 0 and self.OSH % 128 == 0
        assert self.NT % 2 == 0 and self.OG % 2 == 0 and self.NFG % 2 == 0
        assert self.SUP * self.UG <= 6 and self.ICH % 2 == 0


REAL = Cfg()


def build(c, nphases=None, dbg=False, dbg_noselect=False):
    NT, OG, ICH, UCH, UG = c.NT, c.OG, c.ICH, c.UCH, c.UG
    NFG, FCH, FPG, FGS = c.NFG, c.FCH, c.FPG, c.FGS
    OT, NSUP, SUP, G, D, U, QW = c.OT, c.NSUP, c.SUP, c.G, c.D, c.U, c.QW
    NCL, NCH, RPOG, OSH = c.NCL, c.NCH, c.RPOG, c.OSH
    NTOT = OG * NT
    TRND = 16
    NRND = (ICH + TRND - 1) // TRND
    NTT = SUP * UCH
    NRND_A = (NTT + TRND - 1) // TRND

    nc = bacc.Bacc("TRN2", target_bir_lowering=False)

    x_in = nc.dram_tensor("x_in", [c.T, D], F32, kind="ExternalInput").ap()
    y_out = nc.dram_tensor("y_out", [c.T, D], F32, kind="ExternalOutput").ap()
    upgate = nc.dram_tensor("upgate", [2, 128, ICH * U], F32, kind="ExternalInput").ap()
    fuse = nc.dram_tensor("fuse", [1 + G, OG, 128, UCH, 512], F32, kind="ExternalInput").ap()
    gu = nc.dram_tensor("gu", [2, NFG, 128, ICH, FGS], F32, kind="ExternalInput").ap()
    dwt = nc.dram_tensor("dwt", [OG, 128, FCH, 512], F32, kind="ExternalInput").ap()
    qwt = nc.dram_tensor("qwt", [OT, 128, 8, QW], I32, kind="ExternalInput").ap()
    lut4 = nc.dram_tensor("lut4", [OT, 128, NCL], F32, kind="ExternalInput").ap()
    lut8 = nc.dram_tensor("lut8", [OT, 128, NCH], F32, kind="ExternalInput").ap()
    identc = nc.dram_tensor("identc", [128, 128], BF16, kind="ExternalInput").ap()
    clsvec = nc.dram_tensor("clsvec", [128, NCH], F32, kind="ExternalInput").ap()
    shiftv = nc.dram_tensor("shiftv", [128, 32], I32, kind="ExternalInput").ap()
    lp_rep = nc.dram_tensor("lp_rep", [128, U], F32, kind="ExternalInput").ap()
    disc_rep = nc.dram_tensor("disc_rep", [128, G], F32, kind="ExternalInput").ap()
    bias_rep = nc.dram_tensor("bias_rep", [128, D], F32, kind="ExternalInput").ap()
    constv = nc.dram_tensor("constv", [128, 4], F32, kind="ExternalInput").ap()

    xs = nc.dram_tensor("xs", [c.T, D], F32, kind="Internal").ap()
    b05d = nc.dram_tensor("b05d", [128, D], F32, kind="Internal").ap()
    fuse_b = nc.dram_tensor("fuse_b", [1 + G, OG, 128, UCH * 512], BF16, kind="Internal").ap()
    upg_b = nc.dram_tensor("upg_b", [2, 128, ICH * U], BF16, kind="Internal").ap()
    dwt_b = nc.dram_tensor("dwt_b", [OG, 128, FCH * 512], BF16, kind="Internal").ap()
    xlt = nc.dram_tensor("xlt", [NT, 128, ICH * 128], BF16, kind="Internal").ap()
    hts = nc.dram_tensor("hts", [NT, 128, FCH * 128], BF16, kind="Internal").ap()
    cc_in = nc.dram_tensor("cc_in", [2 * OT * 128, D], BF16, kind="Internal").ap()
    wdbg = xdbg = None
    if dbg:
        wdbg = nc.dram_tensor("wdbg", [128, 2 * 2 * ICH * 512], BF16, kind="ExternalOutput").ap()
        xdbg = nc.dram_tensor("xdbg", [128, 2 * ICH * 128], BF16, kind="ExternalOutput").ap()
    cc_out = nc.dram_tensor("cc_out", [NCORES * 2 * OT * 128, D], BF16, kind="Internal").ap()

    from contextlib import ExitStack
    with ExitStack() as ctx:
        e = ctx.enter_context

        AR1 = e(nc.sbuf_tensor("AR1", [128, 65536], BF16))
        AR2 = e(nc.sbuf_tensor("AR2", [128, 24576], BF16))

        def v1(off, n, dt=BF16):
            if dt == BF16:
                return AR1[:, off // 2:off // 2 + n]
            return AR1[:, off // 2:off // 2 + n * 2].bitcast(dt)

        def v2(off, n, dt=BF16):
            if dt == BF16:
                return AR2[:, off // 2:off // 2 + n]
            return AR2[:, off // 2:off // 2 + n * 2].bitcast(dt)

        ident = e(nc.sbuf_tensor("ident", [128, 128], BF16))
        maskb = e(nc.sbuf_tensor("maskb", [128, NT], F32))
        lpb = e(nc.sbuf_tensor("lpb", [128, U], F32))
        expd = e(nc.sbuf_tensor("expd", [128, G], F32))
        clsb = e(nc.sbuf_tensor("clsb", [128, NCH], F32))
        shb = e(nc.sbuf_tensor("shb", [128, 32], I32))
        red = e(nc.sbuf_tensor("red", [128, 2 * ICH + 16], F32))
        sgb = e(nc.sbuf_tensor("sgb", [128, 2 * FGS], BF16))
        hstg = e(nc.sbuf_tensor("hstg", [128, FPG * 128], BF16))
        lutb = e(nc.sbuf_tensor("lutb", [128, NCL + NCH], F32))
        cns = e(nc.sbuf_tensor("cns", [128, 4], F32))

        qwb = v1(0, 8 * QW, I32)
        bitb = v1(8 * QW * 4, D, I32)
        bitf = v1(8 * QW * 4 + 4 * D, D, F32)
        idx8f = v1(8 * QW * 4 + 8 * D, D, F32)
        idx4f = v1(8 * QW * 4 + 12 * D, D, F32)
        idx4b = v1(8 * QW * 4 + 16 * D, D)
        idx8b = v1(8 * QW * 4 + 18 * D, D)
        wacc = v1(8 * QW * 4 + 20 * D, D)
        wtmp = v1(8 * QW * 4 + 22 * D, D)
        waccd = [[v1((28 + 2 * h + a) * D, D // 2) for a in range(2)] for h in range(2)]

        xt2 = v1(0, 2 * D, F32)
        dti = v1(8 * D, D, F32)
        sqt = v1(12 * D, D, F32)

        wsl = v1(0, 2 * 2 * ICH * 512)

        deltab = v1(0, SUP * U, F32)
        pband = v1(4 * SUP * U, SUP * U, F32)
        ub = v1(8 * SUP * U, SUP * U)
        keepb = v1(10 * SUP * U, SUP * U)
        tmpm = v1(12 * SUP * U, SUP * U)
        tmpT = v1(14 * SUP * U, SUP * UCH * 128)
        abst = v1(16 * SUP * U, U, F32)
        fsl = v1(16 * SUP * U + 4 * U, 2 * UCH * 512)
        ugslab = v1(max(16 * SUP * U + 4 * U + 4 * UCH * 512, 16 * D), ICH * U)

        gusl = v1(0, 2 * 2 * ICH * FGS)
        dsl = v1(0, FCH * 512)

        xlb = v2(0, D)
        xstg = v2(2 * D, ICH * 128)

        xsl = v2(4 * D, 2 * ICH * 128)
        bias05v = v2(8 * D, 2 * 512, F32)
        ylb = v2(8 * D + 4096, 2 * 512, F32)
        ydb = v2(8 * D + 8192, 2 * 512, F32)
        ystg = v2(8 * D + 12288, 2 * 512, F32)
        b05s = v2(0, D, F32)
        tmpv = [v2(i * D, D // 2) for i in range(4)]
        prodv = [v2(4 * D + i * D, D // 2) for i in range(8)]

        axsl = v2(0, SUP * ICH * 128)
        ugtile = v2(2 * SUP * ICH * 128, 2 * U)
        astg_a = v2(2 * SUP * ICH * 128 + 2 * 2 * U, SUP * 512, F32)
        astg_b = v2(2 * SUP * ICH * 128 + 2 * 2 * U + SUP * 2048, SUP * 512, F32)
        astg2 = [astg_a, astg_b]

        bxsl = v2(0, 2 * ICH * 128)

        hsl = v2(0, 2 * FCH * 128)
        dstg = v2(2 * 2 * FCH * 128 + 2048, 2 * 512, F32)

        pbk = [e(nc.psum_tensor(f"pb{i}", [128, 512], F32)) for i in range(6)]
        ptrT = e(nc.psum_tensor("ptrT", [128, 2048], BF16))
        ptr = ptrT[:, :]

        def bank(i):
            return pbk[i][:, :]

        def sem(n):
            return e(nc.semaphore(n))

        s_bar = sem("s_bar")
        s_p0 = sem("s_p0"); s_p0c = sem("s_p0c")
        s_dq = sem("s_dq"); s_dv = sem("s_dv"); s_dw = sem("s_dw"); s_cc = sem("s_cc"); s_pc = sem("s_pc")
        s_ds = sem("s_ds"); s_dp = sem("s_dp"); s_de = sem("s_de")
        s_do = sem("s_do"); s_dmg = sem("s_dmg")
        s_lx0 = sem("s_lx0"); s_lx1 = sem("s_lx1")
        s_lc = sem("s_lc"); s_la = sem("s_la")
        s_ltp = sem("s_ltp"); s_lev = sem("s_lev"); s_lw = sem("s_lw")
        s_qw0 = sem("s_qw0"); s_qw1 = sem("s_qw1")
        s_qx0 = sem("s_qx0"); s_qx1 = sem("s_qx1")
        s_qmm = sem("s_qmm"); s_qyl = sem("s_qyl"); s_qdv = sem("s_qdv")
        s_qev = sem("s_qev"); s_qac = sem("s_qac"); s_qbs = sem("s_qbs")
        s_ug0 = sem("s_ug0"); s_ug1 = sem("s_ug1"); s_ax = sem("s_ax")
        s_amm = sem("s_amm"); s_aev = sem("s_aev"); s_adv = sem("s_adv")
        s_atp = sem("s_atp"); s_atv = sem("s_atv")
        s_af0 = sem("s_af0"); s_af1 = sem("s_af1")
        s_apm = sem("s_apm"); s_apy = sem("s_apy"); s_aac0 = sem("s_aac0"); s_aac1 = sem("s_aac1")
        s_gu0 = sem("s_gu0"); s_gu1 = sem("s_gu1")
        s_bx0 = sem("s_bx0"); s_bx1 = sem("s_bx1")
        s_bmm = sem("s_bmm"); s_bsg = sem("s_bsg"); s_bdv = sem("s_bdv")
        s_btp = sem("s_btp"); s_btv = sem("s_btv"); s_bhw = sem("s_bhw")
        s_dl = sem("s_dl"); s_dl2 = sem("s_dl2"); s_dl3 = sem("s_dl3"); s_hx0 = sem("s_hx0"); s_hx1 = sem("s_hx1")
        s_dmm = sem("s_dmm"); s_dcp = sem("s_dcp"); s_dac = sem("s_dac")
        s_out = sem("s_out")

        block = e(nc.Block())

        KD_M, KD_N = 17, 13
        KA_M, KA_N = 4, 3
        LN_MASK = [True, False, True]

        def lkd(i):
            return KD_M if LN_MASK[i] else KD_N

        def lka(i):
            return KA_M if LN_MASK[i] else KA_N

        LC_B = [sum(NT * lkd(j) for j in range(i)) for i in range(4)]
        LA_B = [sum(NT * lka(j) for j in range(i)) for i in range(4)]
        LX_B = [i * (NT // 2) for i in range(4)]
        LT_B = [i * NT * NRND for i in range(4)]
        LW_B = [i * NT for i in range(4)]

        QWSLAB = 2 * ICH * RPOG
        QX_B = [st * OG * (NT // 2) for st in range(2)]
        QWS_B = [st * (OG // 2) * QWSLAB for st in range(2)]
        QG_B = [st * NTOT for st in range(2)]
        QB_B = [st * OG for st in range(2)]

        DVX = 2 + 8 * 34 + 4 + 2
        HD = D // 2
        HCH = HD // 512
        PC = 152
        NOWN = NCH - PC
        OWNH = 2 + 2 * (NOWN - 2)
        OWNOT = 2 * OWNH
        MRG = 4
        SELOT = 2 * (NCL + PC)
        EVOT = 4 * HCH

        AMM_S = 2 * ICH
        AEV_S = 2 * SUP * UG
        UGLD_S = 2 * ICH

        def adv_m(m):
            if m == 0:
                return 0
            ops = 2
            if m >= 2:
                ops += 1
            if m > 4:
                ops += 1
            return ops * SUP

        ADV_S = SUP * UG + 3 * SUP + sum(adv_m(m) for m in range(1 + G))
        ADV_END = []
        _a = SUP * UG + 3 * SUP
        for m in range(1 + G):
            _a += adv_m(m)
            ADV_END.append(_a)

        def p0_g(g):
            g.dma_start(out=ident[:, :], in_=identc).then_inc(s_p0, 16)
            g.dma_start(out=clsb[:, :], in_=clsvec).then_inc(s_p0, 16)
            g.dma_start(out=shb[:, :], in_=shiftv).then_inc(s_p0, 16)
            g.dma_start(out=lpb[:, :], in_=lp_rep).then_inc(s_p0, 16)
            g.dma_start(out=b05s, in_=bias_rep).then_inc(s_p0, 16)
            g.dma_start(out=expd[:, :], in_=disc_rep).then_inc(s_p0, 16)
            g.dma_start(out=cns[:, :], in_=constv).then_inc(s_p0, 16)
            g.dma_start(out=xs, in_=x_in).then_inc(s_p0, 16)
            g.wait_ge(s_p0, 8 * 16)
            g.wait_ge(s_p0c, 2)
            g.dma_start(out=b05d, in_=b05s).then_inc(s_p0, 16)
            g.wait_ge(s_p0, 9 * 16)

        def p0_v(v):
            v.wait_ge(s_p0, 8 * 16)
            v.tensor_scalar(b05s, b05s, 0.5, None, ALU.mult).then_inc(s_p0c, 1)

        def p0_s(sc):
            sc.preload_act_table(AF.Exp)
            sc.wait_ge(s_p0, 8 * 16)
            sc.activation(expd[:, :], expd[:, :], AF.Exp, scale=cns[:, 2:3]).then_inc(s_p0c, 1)

        def dq_g(g):
            for m in range(1 + G):
                g.dma_start(out=fuse_b[m:m + 1].squeeze(0).rearrange("o p q -> (o p) q"),
                            in_=fuse[m:m + 1].squeeze(0).rearrange("o p u q -> (o p) (u q)")).then_inc(s_pc, 16)
            for mat in range(2):
                g.dma_start(out=upg_b[mat:mat + 1].squeeze(0),
                            in_=upgate[mat:mat + 1, :, :].squeeze(0)).then_inc(s_pc, 16)
            with g.Fori(0, OG) as ogq:
                g.dma_start(out=dwt_b[ds(ogq, 1)].squeeze(0),
                            in_=dwt[ds(ogq, 1)].squeeze(0).rearrange("p f q -> p (f q)")).then_inc(s_pc, 16)
            with g.Fori(0, OT) as ot:
                with g.If(ot >= 1):
                    g.wait_ge(s_dv, ot * DVX)
                    g.wait_ge(s_ds, ot * SELOT)
                    g.wait_ge(s_do, ot * OWNOT)
                g.dma_start(out=qwb, in_=qwt[ds(ot, 1)].squeeze(0).rearrange("p a b -> p (a b)")).then_inc(s_dq, 16)
                g.dma_start(out=lutb[:, 0:NCL], in_=lut4[ds(ot, 1)].squeeze(0)).then_inc(s_dq, 16)
                g.dma_start(out=lutb[:, NCL:NCL + NCH], in_=lut8[ds(ot, 1)].squeeze(0)).then_inc(s_dq, 16)
                g.wait_ge(s_de, EVOT * (ot + 1))
                g.wait_ge(s_dmg, MRG * (ot + 1))
                g.dma_start(out=cc_in[ds(ot * 128, 128), :], in_=wacc).then_inc(s_dw, 16)
                g.dma_start(out=cc_in[ds((OT + ot) * 128, 128), :], in_=wtmp).then_inc(s_dw, 16)
            g.wait_ge(s_dw, 32 * OT)
            g.collective_compute(
                "AllGather", ALU.bypass,
                replica_groups=[list(range(NCORES))],
                ins=[cc_in], outs=[cc_out],
            ).then_inc(s_cc)
            g.wait_ge(s_cc, 1)
            g.wait_ge(s_pc, 16 * (3 + G + OG))

        def dq_v(v):
            bitb3 = bitb.rearrange("p (w s) -> p w s", s=32)
            with v.Fori(0, OT) as ot:
                bb = ot * DVX
                v.wait_ge(s_dq, 48 * (ot + 1))
                j = [0]

                def step(inst):
                    inst.then_inc(s_dv, 1)
                    j[0] += 1

                def wj():
                    v.wait_ge(s_dv, bb + j[0])

                step(v.memset(idx8f, 0.0))
                step(v.memset(idx4f, 0.0))
                for p in range(8):
                    npre = 2 + p * 34 + (min(p, 4))
                    with v.Fori(0, 32) as s:
                        v.wait_ge(s_dv, bb + npre + s)
                        v.tensor_scalar(bitb3[:, :, ds(s, 1)].squeeze(2),
                                        qwb[:, p * QW:(p + 1) * QW],
                                        shb[:, ds(s, 1)], None,
                                        ALU.bitwise_and).then_inc(s_dv, 1)
                    j[0] += 32
                    wj()
                    step(v.tensor_scalar(bitf, bitb, 0, None, ALU.not_equal))
                    wj()
                    step(v.scalar_tensor_tensor(idx8f, bitf, float(2 ** p), idx8f, ALU.mult, ALU.add))
                    wj()
                    if p < 4:
                        step(v.scalar_tensor_tensor(idx4f, bitf, float(2 ** p), idx4f, ALU.mult, ALU.add))
                        wj()
                step(v.tensor_copy(idx4b, idx4f))
                step(v.tensor_copy(idx8b, idx8f))
                assert j[0] == DVX, (j[0], DVX)
                wj()
                bsel = ot * SELOT
                bown = ot * OWNOT

                for h in (0, 1):
                    hb = bsel + h * (NCL + PC)
                    ob = bown + h * OWNH
                    idx4h = idx4b[:, h * HD:(h + 1) * HD]
                    idx8h = idx8b[:, h * HD:(h + 1) * HD]

                    seq = [("d", PC), ("d", PC + 1)]
                    for k in range(PC + 2, NCH):
                        seq.append(("s", k))
                        if k >= PC + 3:
                            seq.append(("a", k - 1))
                    seq.append(("a", NCH - 1))
                    assert len(seq) == OWNH
                    pos = {op: i for i, op in enumerate(seq)}
                    own_i = [0]

                    def emit_own():
                        if own_i[0] >= len(seq):
                            return
                        typ, k = seq[own_i[0]]
                        own_i[0] += 1
                        a = k & 1
                        ccol = clsb[:, k:k + 1]
                        lcol = lutb[:, NCL + k:NCL + k + 1]
                        if typ == "d":
                            with v.If(ot >= 1):
                                v.wait_ge(s_dmg, (ot - 1) * MRG + 2 * (h + 1))
                            v.tensor_scalar(waccd[h][a], idx8h, ccol, lcol,
                                            ALU.is_equal, ALU.mult).then_inc(s_do, 1)
                        elif typ == "s":
                            dep = pos.get(("a", k - 4))
                            v.wait_ge(s_do, ob + (dep + 1 if dep is not None else 0))
                            v.tensor_scalar(tmpv[k & 3], idx8h, ccol, lcol,
                                            ALU.is_equal, ALU.mult).then_inc(s_do, 1)
                        else:
                            pk = ("a", k - 2) if ("a", k - 2) in pos else ("d", k - 2)
                            dep = max(pos[("s", k)], pos[pk])
                            v.wait_ge(s_do, ob + dep + 1)
                            v.tensor_tensor(waccd[h][a], waccd[h][a], tmpv[k & 3],
                                            ALU.add).then_inc(s_do, 1)

                    credit = [0.0]
                    ratio = OWNH / (NCL + PC)
                    for mi, (idxh_, l0, cnt) in enumerate(((idx4h, 0, NCL), (idx8h, NCL, PC))):
                        base = hb + (0 if mi == 0 else NCL)
                        for k in range(cnt):
                            n = base + k
                            off = h * (NCL + PC) + (0 if mi == 0 else NCL) + k
                            if off < 8:
                                with v.If(n >= 8):
                                    v.wait_ge(s_dp, n - 7)
                            else:
                                v.wait_ge(s_dp, n - 7)
                            v.tensor_scalar(prodv[k & 7], idxh_,
                                            clsb[:, k:k + 1], lutb[:, l0 + k:l0 + k + 1],
                                            ALU.is_equal, ALU.mult).then_inc(s_ds, 1)
                            credit[0] += ratio
                            while credit[0] >= 1.0:
                                emit_own()
                                credit[0] -= 1.0
                    while own_i[0] < len(seq):
                        emit_own()

                for h in (0, 1):
                    hs = slice(h * HD, (h + 1) * HD)
                    v.wait_ge(s_de, ot * EVOT + (2 * h + 2) * HCH)
                    v.tensor_tensor(wtmp[:, hs], wtmp[:, hs], waccd[h][0], ALU.add).then_inc(s_dmg, 1)
                    v.wait_ge(s_dmg, ot * MRG + 2 * h + 1)
                    v.tensor_tensor(wtmp[:, hs], wtmp[:, hs], waccd[h][1], ALU.add).then_inc(s_dmg, 1)

        def dq_t(te):
            with te.Fori(0, OT) as ot:
                bsel = ot * SELOT
                for h in (0, 1):
                    for mi, cnt in ((0, NCL), (1, PC)):
                        sec = h * 2 + mi
                        base = bsel + h * (NCL + PC) + (0 if mi == 0 else NCL)
                        gsec = ot * 4 + sec
                        if sec == 0:
                            with te.If(ot >= 1):
                                te.wait_ge(s_de, HCH * gsec)
                        else:
                            te.wait_ge(s_de, HCH * gsec)

                        def grp(k, first, last):
                            te.wait_ge(s_ds, base + k + 1)
                            lastmm = None
                            for j in range(HCH):
                                lastmm = te.matmul(bank(j), ident[:, :],
                                                   prodv[k & 7][:, j * 512:(j + 1) * 512],
                                                   start=first, stop=last)
                            lastmm.then_inc(s_dp, 1)

                        nch8 = cnt // 8
                        assert cnt % 8 == 0
                        for kk in range(8):
                            grp(kk, kk == 0, False)
                        if nch8 > 2:
                            with te.Fori(1, nch8 - 1) as c:
                                for kk in range(8):
                                    n = base + c * 8 + kk
                                    te.wait_ge(s_ds, n + 1)
                                    lastmm = None
                                    for j in range(HCH):
                                        lastmm = te.matmul(bank(j), ident[:, :],
                                                           prodv[kk][:, j * 512:(j + 1) * 512],
                                                           start=False, stop=False)
                                    lastmm.then_inc(s_dp, 1)
                        for kk in range(8):
                            k = (nch8 - 1) * 8 + kk
                            grp(k, False, kk == 7)

        def dq_s(sc):
            with sc.Fori(0, OT) as ot:
                with sc.If(ot >= 1):
                    sc.wait_ge(s_dw, 32 * ot)
                for h in (0, 1):
                    for mi, cnt in ((0, NCL), (1, PC)):
                        sec_end = ot * SELOT + h * (NCL + PC) + (NCL if mi == 0 else NCL + PC)
                        sc.wait_ge(s_dp, sec_end)
                        dst = (wacc, wtmp)[mi]
                        for j in range(HCH):
                            sc.activation(dst[:, h * HD + j * 512:h * HD + j * 512 + 512],
                                          bank(j), AF.Copy).then_inc(s_de, 1)

        def ln_xload(sy, si, wg=None):
            KD = lkd(si)
            with sy.Fori(0, NT) as t:
                with sy.If(t >= 2):
                    sy.wait_ge(s_lc, LC_B[si] + (t - 2) * KD + 4)
                with sy.register(f"lns{si}") as r:
                    sy.reg_alu(r, sy.to_reg(t), 1, ALU.bitwise_and)
                    with sy.If_eq(r, 0):
                        sy.dma_start(out=xt2[:, 0:D], in_=xs[ds(t * 128, 128), :]).then_inc(s_lx0, 16)
                    with sy.Else():
                        sy.dma_start(out=xt2[:, D:2 * D], in_=xs[ds(t * 128, 128), :]).then_inc(s_lx1, 16)
                if wg is not None:
                    wg(sy, t)

        def ln_v(v, si, wv=None):
            KD, KA = lkd(si), lka(si)
            mask = LN_MASK[si]
            with v.Fori(0, NT) as t:
                b = LC_B[si] + t * KD
                ba = LA_B[si] + t * KA
                with v.register(f"lnv{si}") as r:
                    v.reg_alu(r, v.to_reg(t), 1, ALU.bitwise_and)
                    for slot in (0, 1):
                        with v.If_eq(r, slot):
                            v.wait_ge((s_lx0, s_lx1)[slot], 16 * (LX_B[si] + t // 2 + 1))
                            x = xt2[:, slot * D:(slot + 1) * D]
                            v.tensor_reduce(red[:, 0:ICH], x.rearrange("p (a q) -> p a q", q=128), AX.X, ALU.add).then_inc(s_lc, 1)
                            v.wait_ge(s_lc, b + 1)
                            v.tensor_reduce(red[:, ICH:ICH + 1], red[:, 0:ICH], AX.X, ALU.add).then_inc(s_lc, 1)
                            v.wait_ge(s_lc, b + 2)
                            v.tensor_scalar(red[:, ICH + 1:ICH + 2], red[:, ICH:ICH + 1], 1.0 / D, None, ALU.mult).then_inc(s_lc, 1)
                            v.wait_ge(s_lc, b + 3)
                            v.tensor_scalar(dti, x, red[:, ICH + 1:ICH + 2], None, ALU.subtract).then_inc(s_lc, 1)
                            v.wait_ge(s_la, ba + 1)
                            v.tensor_reduce(red[:, 0:ICH], sqt.rearrange("p (a q) -> p a q", q=128), AX.X, ALU.add).then_inc(s_lc, 1)
                            v.wait_ge(s_lc, b + 5)
                            v.tensor_reduce(red[:, ICH + 2:ICH + 3], red[:, 0:ICH], AX.X, ALU.add).then_inc(s_lc, 1)
                            v.wait_ge(s_la, ba + 2)
                            v.reciprocal(red[:, ICH + 4:ICH + 5], red[:, ICH + 3:ICH + 4]).then_inc(s_lc, 1)
                            v.wait_ge(s_lc, b + 7)
                            v.tensor_scalar(red[:, ICH + 5:ICH + 6], red[:, ICH + 2:ICH + 3], 1.0 / D, LN_EPS, ALU.mult, ALU.add).then_inc(s_lc, 1)
                            v.wait_ge(s_la, ba + 3)
                            v.wait_ge(s_lc, b + 8)
                            v.tensor_tensor(red[:, ICH + 7:ICH + 8], red[:, ICH + 6:ICH + 7], red[:, ICH + 5:ICH + 6], ALU.mult).then_inc(s_lc, 1)
                            v.wait_ge(s_lc, b + 9)
                            v.tensor_scalar(red[:, ICH + 8:ICH + 9], red[:, ICH + 7:ICH + 8], -0.5, 1.5, ALU.mult, ALU.add).then_inc(s_lc, 1)
                            v.wait_ge(s_lc, b + 10)
                            v.tensor_tensor(red[:, ICH + 9:ICH + 10], red[:, ICH + 4:ICH + 5], red[:, ICH + 8:ICH + 9], ALU.mult).then_inc(s_lc, 1)
                            v.wait_ge(s_lc, b + 11)
                            v.tensor_scalar(sqt, dti, red[:, ICH + 9:ICH + 10], None, ALU.mult).then_inc(s_lc, 1)
                            v.wait_ge(s_lc, b + 12)
                            with v.If(t >= 1):
                                v.wait_ge(s_ltp, LT_B[si] + t * NRND)
                            v.tensor_copy(xlb, sqt).then_inc(s_lc, 1)
                            if mask:
                                v.wait_ge(s_la, ba + 4)
                                v.tensor_reduce(red[:, ICH:2 * ICH], dti.rearrange("p (a q) -> p a q", q=128), AX.X, ALU.add).then_inc(s_lc, 1)
                                v.wait_ge(s_lc, b + 14)
                                v.tensor_scalar(red[:, ICH:2 * ICH], red[:, ICH:2 * ICH], -128.0, None, ALU.add).then_inc(s_lc, 1)
                                v.wait_ge(s_lc, b + 15)
                                v.tensor_reduce(red[:, 2 * ICH:2 * ICH + 1], red[:, ICH:2 * ICH], AX.X, ALU.add).then_inc(s_lc, 1)
                                v.wait_ge(s_lc, b + 16)
                                v.tensor_scalar(maskb[:, ds(t, 1)], red[:, 2 * ICH:2 * ICH + 1], c.THB, None, ALU.is_gt).then_inc(s_lc, 1)
                for rnd in range(NRND):
                    ntile = min(TRND, ICH - rnd * TRND)
                    v.wait_ge(s_ltp, LT_B[si] + t * NRND + rnd + 1)
                    if rnd == 0:
                        with v.If(t >= 1):
                            v.wait_ge(s_lw, 16 * (LW_B[si] + t))
                    last = None
                    for k in range(ntile):
                        kk = rnd * TRND + k
                        last = v.tensor_copy(xstg[:, kk * 128:(kk + 1) * 128], ptr[:, k * 128:(k + 1) * 128])
                    last.then_inc(s_lev, 1)
                if wv is not None:
                    wv(v, t)

        def ln_s(sc, si, ws=None):
            KD, KA = lkd(si), lka(si)
            mask = LN_MASK[si]
            sc.preload_act_table(AF.Sqrt)
            with sc.Fori(0, NT) as t:
                b = LC_B[si] + t * KD
                sc.wait_ge(s_lc, b + 4)
                sc.activation(sqt, dti, AF.Square).then_inc(s_la, 1)
                sc.wait_ge(s_lc, b + 6)
                sc.activation(red[:, ICH + 3:ICH + 4], red[:, ICH + 2:ICH + 3], AF.Sqrt, bias=cns[:, 0:1], scale=cns[:, 1:2]).then_inc(s_la, 1)
                sc.wait_ge(s_lc, b + 7)
                sc.activation(red[:, ICH + 6:ICH + 7], red[:, ICH + 4:ICH + 5], AF.Square).then_inc(s_la, 1)
                if mask:
                    sc.wait_ge(s_lc, b + 12)
                    sc.activation(dti, sqt, AF.Square).then_inc(s_la, 1)
                sc.wait_ge(s_lev, LT_B[si] + (t + 1) * NRND)
                sc.dma_start(out=xlt[ds(t, 1)].squeeze(0), in_=xstg).then_inc(s_lw, 16)
                if ws is not None:
                    ws(sc, t)
            sc.wait_ge(s_lw, 16 * (LW_B[si] + NT))

        def ln_t(te, si, weave=None):
            KD = lkd(si)
            with te.Fori(0, NT) as t:
                b = LC_B[si] + t * KD
                te.wait_ge(s_lc, b + 13)
                for rnd in range(NRND):
                    ntile = min(TRND, ICH - rnd * TRND)
                    te.wait_ge(s_lev, LT_B[si] + t * NRND + rnd)
                    last = None
                    for k in range(ntile):
                        kk = rnd * TRND + k
                        last = te.transpose(ptr[:, k * 128:(k + 1) * 128], xlb[:, kk * 128:(kk + 1) * 128], ident[:, :])
                    last.then_inc(s_ltp, 1)
                if weave is not None:
                    weave(te, t)

        def q_sync(sy, st):
            si = 0 if st == 0 else 2
            lnd = LC_B[si] + NT * lkd(si)
            with sy.Fori(0, OG) as og:
                with sy.If(og >= 2):
                    sy.wait_ge(s_qev, QG_B[st] + (og - 1) * NT)
                with sy.register(f"qs{st}") as r:
                    sy.reg_alu(r, sy.to_reg(og), 1, ALU.bitwise_and)
                    for slot in (0, 1):
                        with sy.If_eq(r, slot):
                            if slot == 1:
                                sy.wait_ge(s_lc, lnd)
                            sy.dma_start(out=bias05v[:, slot * 512:(slot + 1) * 512],
                                         in_=b05d[:, ts(og, 512)]).then_inc(s_qbs, 16)
                            swl = (s_qw0, s_qw1)[slot]
                            for mat in range(2):
                                for ich in range(ICH):
                                    for rr in range(RPOG):
                                        src = cc_out[ds((og * RPOG + rr) * (2 * OT * 128) + mat * OT * 128, OT * 128),
                                                     ich * 128:(ich + 1) * 128]
                                        dof = (1 - slot) * 2 * ICH * 512 + (mat * ICH + ich) * 512 + rr * OSH
                                        sy.dma_start_transpose(wsl[:, dof:dof + OSH], src).then_inc(swl, 16)

        def q_xload(sc, st, og, t, n, tag):
            with sc.If(n >= 2):
                sc.wait_ge(s_qmm, QG_B[st] + n - 1)
            with sc.register(f"qc{st}{tag}") as r:
                sc.reg_alu(r, sc.to_reg(t), 1, ALU.bitwise_and)
                for slot in (0, 1):
                    with sc.If_eq(r, slot):
                        sxl = (s_qx0, s_qx1)[slot]
                        sc.dma_start(out=xsl[:, slot * ICH * 128:(slot + 1) * ICH * 128],
                                     in_=xlt[ds(t, 1)].squeeze(0)).then_inc(sxl, 16)

        def qs_weave(sc, st, t):
            with sc.If(t >= 2):
                tm, _ = s_assert_within(t - 2, 0, NT - 1)
                q_xload(sc, st, 0, tm, tm, "w")

        def q_scalar(sc, st, woven_og0=False):
            if woven_og0:
                with sc.Fori(NT - 2, NT) as t:
                    q_xload(sc, st, 0, t, t, "a")
                with sc.Fori(1, OG) as og:
                    with sc.Fori(0, NT) as t:
                        q_xload(sc, st, og, t, og * NT + t, "b")
            else:
                with sc.Fori(0, OG) as og:
                    with sc.Fori(0, NT) as t:
                        q_xload(sc, st, og, t, og * NT + t, "c")

        def q_group(te, st, og, oslot, t, n, tslot):
            sxl = (s_qx0, s_qx1)[tslot]
            te.wait_ge(sxl, 16 * (QX_B[st] + og * (NT // 2) + t // 2 + 1))
            with te.If(n >= 2):
                te.wait_ge(s_qev, QG_B[st] + n - 1)
            for mat in range(2):
                last = None
                for ich in range(ICH):
                    last = te.matmul(
                        bank(tslot * 2 + mat),
                        xsl[:, (tslot * ICH + ich) * 128:(tslot * ICH + ich) * 128 + 128],
                        wsl[:, (1 - oslot) * 2 * ICH * 512 + (mat * ICH + ich) * 512:
                               (1 - oslot) * 2 * ICH * 512 + (mat * ICH + ich) * 512 + 512],
                        start=(ich == 0), stop=(ich == ICH - 1))
                if mat == 1:
                    last.then_inc(s_qmm, 1)

        def q_weave(te, st, t):
            with te.If(t >= 2):
                tm, _ = s_assert_within(t - 2, 0, NT - 1)
                te.wait_ge(s_qw0, 16 * (QWS_B[st] + QWSLAB))
                with te.register(f"qw{st}") as rt:
                    te.reg_alu(rt, te.to_reg(t), 1, ALU.bitwise_and)
                    for tslot in (0, 1):
                        with te.If_eq(rt, tslot):
                            q_group(te, st, 0, 0, tm, tm, tslot)

        def q_t_loop(te, st, og, oslot, t0=0, tag=""):
            swl = (s_qw0, s_qw1)[oslot]
            te.wait_ge(swl, 16 * (QWS_B[st] + (og // 2 + 1) * QWSLAB))
            with te.Fori(t0, NT) as t:
                n = og * NT + t
                with te.register(f"qtt{st}{oslot}{tag}") as rt:
                    te.reg_alu(rt, te.to_reg(t), 1, ALU.bitwise_and)
                    for tslot in (0, 1):
                        with te.If_eq(rt, tslot):
                            q_group(te, st, og, oslot, t, n, tslot)

        def q_tensor(te, st, woven_og0=False):
            if woven_og0:
                q_t_loop(te, st, 0, 0, t0=NT - 2, tag="t")
                with te.Fori(1, OG) as og:
                    with te.register(f"qto{st}") as ro:
                        te.reg_alu(ro, te.to_reg(og), 1, ALU.bitwise_and)
                        with te.If_eq(ro, 0):
                            q_t_loop(te, st, og, 0)
                        with te.Else():
                            q_t_loop(te, st, og, 1)
            else:
                with te.Fori(0, OG) as og:
                    with te.register(f"qto{st}") as ro:
                        te.reg_alu(ro, te.to_reg(og), 1, ALU.bitwise_and)
                        with te.If_eq(ro, 0):
                            q_t_loop(te, st, og, 0)
                        with te.Else():
                            q_t_loop(te, st, og, 1)

        def q_ev(v, st, og, t, n, tag, ogslot=None):
            bq = (QG_B[st] + n) * 4
            v.wait_ge(s_qmm, QG_B[st] + n + 1)
            v.wait_ge(s_qbs, 16 * (QB_B[st] + og + 1))
            with v.If(n >= 2):
                v.wait_ge(s_qac, 16 * (QG_B[st] + n))
            with v.register(f"qv{st}{tag}") as r:
                v.reg_alu(r, v.to_reg(t), 1, ALU.bitwise_and)
                for slot in (0, 1):
                    with v.If_eq(r, slot):
                        yl = ylb[:, slot * 512:(slot + 1) * 512]
                        yd = ydb[:, slot * 512:(slot + 1) * 512]
                        yg = ystg[:, slot * 512:(slot + 1) * 512]
                        v.tensor_copy(yl, bank(slot * 2)).then_inc(s_qdv, 1)
                        v.wait_ge(s_qdv, bq + 1)
                        v.tensor_tensor(yd, bank(slot * 2 + 1), yl, ALU.subtract).then_inc(s_qdv, 1)
                        v.wait_ge(s_qdv, bq + 2)
                        v.scalar_tensor_tensor(yd, yd, maskb[:, ds(t, 1)], yl, ALU.mult, ALU.add).then_inc(s_qdv, 1)
                        v.wait_ge(s_qdv, bq + 3)
                        if ogslot is not None:
                            v.scalar_tensor_tensor(yg, yd, 0.5, bias05v[:, ogslot * 512:(ogslot + 1) * 512],
                                                   ALU.mult, ALU.add).then_inc(s_qdv, 1)
                        else:
                            with v.register(f"qo{st}{tag}{slot}") as ro:
                                v.reg_alu(ro, v.to_reg(og), 1, ALU.bitwise_and)
                                for os_ in (0, 1):
                                    with v.If_eq(ro, os_):
                                        v.scalar_tensor_tensor(yg, yd, 0.5, bias05v[:, os_ * 512:(os_ + 1) * 512],
                                                               ALU.mult, ALU.add).then_inc(s_qdv, 1)
                        v.wait_ge(s_qdv, bq + 4)
            v.engine_nop().then_inc(s_qev, 1)

        def qv_weave(v, st, t):
            with v.If(t >= 2):
                tm, _ = s_assert_within(t - 2, 0, NT - 1)
                q_ev(v, st, 0, tm, tm, "w", ogslot=0)

        def q_vector(v, st, woven_og0=False):
            if woven_og0:
                with v.Fori(NT - 2, NT) as t:
                    q_ev(v, st, 0, t, t, "a", ogslot=0)
                with v.Fori(1, OG) as og:
                    with v.Fori(0, NT) as t:
                        q_ev(v, st, og, t, og * NT + t, "b")
            else:
                with v.Fori(0, OG) as og:
                    with v.Fori(0, NT) as t:
                        q_ev(v, st, og, t, og * NT + t, "c")

        def q_acc(g, st, og, t, n, tag):
            g.wait_ge(s_qev, QG_B[st] + n + 1)
            with g.register(f"qg{st}{tag}") as r:
                g.reg_alu(r, g.to_reg(t), 1, ALU.bitwise_and)
                for slot in (0, 1):
                    with g.If_eq(r, slot):
                        g.dma_start(out=xs[ds(t * 128, 128), ts(og, 512)],
                                    in_=ystg[:, slot * 512:(slot + 1) * 512],
                                    accum_op=ALU.add).then_inc(s_qac, 16)

        def qg_weave(g, st, t):
            if NT <= 4:
                return
            with g.If(t >= 4):
                tm, _ = s_assert_within(t - 4, 0, NT - 1)
                q_acc(g, st, 0, tm, tm, "w")

        def q_gpsimd(g, st, woven_og0=False):
            if woven_og0:
                with g.Fori(NT - 4, NT) as t:
                    q_acc(g, st, 0, t, t, "a")
                with g.Fori(1, OG) as og:
                    with g.Fori(0, NT) as t:
                        q_acc(g, st, og, t, og * NT + t, "b")
            else:
                with g.Fori(0, OG) as og:
                    with g.Fori(0, NT) as t:
                        q_acc(g, st, og, t, og * NT + t, "c")
            g.wait_ge(s_qac, 16 * (QG_B[st] + NTOT))

        def a_sync(sy):
            with sy.Fori(0, NSUP) as sp:
                for mat in range(2):
                    nld = sp * 2 + mat
                    with sy.If(nld >= 1):
                        sy.wait_ge(s_amm, (sp * 2 + mat) * ICH)
                    sy.dma_start(out=ugslab, in_=upg_b[mat:mat + 1].squeeze(0)).then_inc(s_ug0, 16)
                sy.wait_ge(s_lc, LC_B[1] + NT * lkd(1))
                for m in range(1 + G):
                    with sy.Fori(0, OG) as og:
                        nf = sp * (1 + G) * OG + m * OG + og
                        with sy.If(nf >= 2):
                            sy.wait_ge(s_apm, (nf - 1) * SUP)
                        with sy.register(f"af{m}") as r:
                            sy.reg_alu(r, sy.to_reg(og), 1, ALU.bitwise_and)
                            for slot in (0, 1):
                                with sy.If_eq(r, slot):
                                    saf = (s_af0, s_af1)[slot]
                                    sy.dma_start(out=fsl[:, slot * UCH * 512:(slot + 1) * UCH * 512],
                                                 in_=fuse_b[m:m + 1].squeeze(0)[ds(og, 1)].squeeze(0)).then_inc(saf, 16)

        def a_gpsimd(g):
            with g.Fori(0, NSUP) as sp:
                for m in range(1 + G):
                    with g.Fori(0, OG) as og:
                        nf = sp * (1 + G) * OG + m * OG + og
                        g.wait_ge(s_apy, (nf + 1) * SUP)
                        with g.register(f"ac{m}") as r:
                            g.reg_alu(r, g.to_reg(og), 1, ALU.bitwise_and)
                            for slot in (0, 1):
                                with g.If_eq(r, slot):
                                    g.dma_start(out=xs[ds(sp * SUP * 128, SUP * 128), ts(og, 512)].rearrange("(jj p) o -> p jj o", p=128),
                                                in_=astg2[slot].rearrange("p (jj o) -> p jj o", jj=SUP),
                                                accum_op=ALU.add).then_inc((s_aac0, s_aac1)[slot], 16)
            NFTOT = NSUP * (1 + G) * OG
            g.wait_ge(s_aac0, 16 * ((NFTOT + 1) // 2))
            g.wait_ge(s_aac1, 16 * (NFTOT // 2))

        def a_scalar(sc):
            with sc.Fori(0, NSUP) as sp:
                with sc.If(sp >= 1):
                    sc.wait_ge(s_amm, sp * AMM_S)
                with sc.Fori(0, SUP) as jj:
                    t = sp * SUP + jj
                    sc.dma_start(out=axsl.rearrange("p (jj q) -> p jj q", jj=SUP)[:, ds(jj, 1)].squeeze(1),
                                 in_=xlt[ds(t, 1)].squeeze(0)).then_inc(s_ax, 16)
                sc.wait_ge(s_amm, sp * AMM_S + ICH)
                with sc.If(sp >= 1):
                    sc.wait_ge(s_adv, sp * ADV_S)
                    sc.wait_ge(s_atp, sp * (1 + G) * NRND_A)
                last = None
                for jj in range(SUP):
                    for ug in range(UG):
                        last = sc.activation(ub[:, (jj * UG + ug) * 512:(jj * UG + ug) * 512 + 512],
                                             bank(jj * UG + ug), AF.Copy)
                last.then_inc(s_aev, SUP * UG)
                for m in range(1 + G):
                    with sc.Fori(0, OG) as og:
                        nf = sp * (1 + G) * OG + m * OG + og
                        with sc.register(f"as{m}") as r:
                            sc.reg_alu(r, sc.to_reg(og), 1, ALU.bitwise_and)
                            for oslot in (0, 1):
                                with sc.If_eq(r, oslot):
                                    with sc.If(nf >= 2):
                                        sc.wait_ge((s_aac0, s_aac1)[oslot], 16 * (nf // 2))
                                    for jj in range(SUP):
                                        nap = nf * SUP + jj
                                        sc.wait_ge(s_apm, nap + 1)
                                        sc.activation(astg2[oslot][:, jj * 512:(jj + 1) * 512], bank(2 + (SUP * oslot + jj) % 4), AF.Copy).then_inc(s_apy, 1)

        def a_tensor(te):
            with te.Fori(0, NSUP) as sp:
                te.wait_ge(s_ax, 16 * (sp + 1) * SUP)
                for mat in range(2):
                    te.wait_ge(s_ug0, 16 * (sp * 2 + mat + 1))
                    for ich in range(ICH):
                        if ich == 0:
                            if mat == 0:
                                with te.If(sp >= 1):
                                    te.wait_ge(s_aev, sp * AEV_S)
                                    te.wait_ge(s_apy, sp * (1 + G) * OG * SUP)
                            else:
                                te.wait_ge(s_aev, sp * AEV_S + SUP * UG)
                        last = None
                        for jj in range(SUP):
                            for ug in range(UG):
                                last = te.matmul(bank(jj * UG + ug),
                                                 axsl[:, (jj * ICH + ich) * 128:(jj * ICH + ich) * 128 + 128],
                                                 ugslab[:, ich * U + ug * 512:ich * U + ug * 512 + 512],
                                                 start=(ich == 0), stop=(ich == ICH - 1))
                        last.then_inc(s_amm, 1)
                for m in range(1 + G):
                    src = ub if m == 0 else tmpm
                    if m == 0:
                        te.wait_ge(s_aev, (sp + 1) * AEV_S)
                    else:
                        te.wait_ge(s_adv, sp * ADV_S + ADV_END[m])
                    ngrp = sp * (1 + G) + m
                    for rnd in range(NRND_A):
                        ntile = min(TRND, NTT - rnd * TRND)
                        te.wait_ge(s_atv, ngrp * NRND_A + rnd)
                        last = None
                        for k in range(ntile):
                            kk = rnd * TRND + k
                            last = te.transpose(ptr[:, k * 128:(k + 1) * 128], src[:, kk * 128:(kk + 1) * 128], ident[:, :])
                        last.then_inc(s_atp, 1)
                    te.wait_ge(s_atv, (ngrp + 1) * NRND_A)
                    with te.Fori(0, OG) as og:
                        nf = sp * (1 + G) * OG + m * OG + og
                        with te.register(f"am{m}") as r:
                            te.reg_alu(r, te.to_reg(og), 1, ALU.bitwise_and)
                            for oslot in (0, 1):
                                with te.If_eq(r, oslot):
                                    saf = (s_af0, s_af1)[oslot]
                                    te.wait_ge(saf, 16 * (nf // 2 + 1))
                                    for jj in range(SUP):
                                        nap = nf * SUP + jj
                                        with te.If(nap >= 4):
                                            te.wait_ge(s_apy, nap - 3)
                                        last = None
                                        for uch in range(UCH):
                                            last = te.matmul(bank(2 + (SUP * oslot + jj) % 4),
                                                             tmpT[:, (jj * UCH + uch) * 128:(jj * UCH + uch) * 128 + 128],
                                                             fsl[:, (oslot * UCH + uch) * 512:(oslot * UCH + uch) * 512 + 512],
                                                             start=(uch == 0), stop=(uch == UCH - 1))
                                        last.then_inc(s_apm, 1)

        def a_vector(v):
            with v.Fori(0, NSUP) as sp:
                bD = sp * ADV_S
                j = [0]

                def step(inst):
                    inst.then_inc(s_adv, 1)
                    j[0] += 1

                def wj():
                    v.wait_ge(s_adv, bD + j[0])

                v.wait_ge(s_amm, (sp + 1) * AMM_S)
                for jj in range(SUP):
                    for ug in range(UG):
                        step(v.tensor_tensor(deltab[:, (jj * UG + ug) * 512:(jj * UG + ug) * 512 + 512],
                                             bank(jj * UG + ug), lpb[:, ug * 512:(ug + 1) * 512], ALU.subtract))
                v.engine_nop().then_inc(s_aev, SUP * UG)
                wj()
                for jj in range(SUP):
                    du = deltab[:, jj * U:(jj + 1) * U]
                    step(v.tensor_scalar(abst, du, -1.0, None, ALU.mult))
                    wj()
                    step(v.tensor_tensor(abst, abst, du, ALU.max))
                    wj()
                    step(v.tensor_scalar(keepb[:, jj * U:(jj + 1) * U], abst, 2.5, None, ALU.is_le))
                    wj()
                def evac_m(m):
                    ngrp = sp * (1 + G) + m
                    v.wait_ge(s_apm, ngrp * OG * SUP)
                    for rnd in range(NRND_A):
                        ntile = min(TRND, NTT - rnd * TRND)
                        v.wait_ge(s_atp, ngrp * NRND_A + rnd + 1)
                        last = None
                        for k in range(ntile):
                            kk = rnd * TRND + k
                            last = v.tensor_copy(tmpT[:, kk * 128:(kk + 1) * 128], ptr[:, k * 128:(k + 1) * 128])
                        last.then_inc(s_atv, 1)

                for m in range(1, 1 + G):
                    if m >= 2:
                        v.wait_ge(s_atp, (sp * (1 + G) + m) * NRND_A)
                    for jj in range(SUP):
                        du = deltab[:, jj * U:(jj + 1) * U]
                        pu = pband[:, jj * U:(jj + 1) * U]
                        tu = tmpm[:, jj * U:(jj + 1) * U]
                        uu = ub[:, jj * U:(jj + 1) * U]
                        ku = keepb[:, jj * U:(jj + 1) * U]
                        if m >= 2:
                            step(v.tensor_tensor(pu, pu if m > 2 else du, du, ALU.mult))
                            wj()
                        psrc = du if m == 1 else pu
                        step(v.tensor_scalar(tu, psrc, expd[:, m - 1:m], None, ALU.mult))
                        wj()
                        step(v.tensor_tensor(tu, tu, uu, ALU.mult))
                        wj()
                        if m > 4:
                            step(v.tensor_tensor(tu, tu, ku, ALU.mult))
                            wj()
                    evac_m(m - 1)
                evac_m(G)
                assert j[0] == ADV_S, (j[0], ADV_S)

        def bi_sync(sy):
            with sy.Fori(0, NFG) as fg:
                with sy.Fori(0, NT) as t:
                    n = fg * NT + t
                    with sy.If(n >= 2):
                        sy.wait_ge(s_bmm, 2 * (n - 1))
                    with sy.register("bsy") as r:
                        sy.reg_alu(r, sy.to_reg(t), 1, ALU.bitwise_and)
                        for slot in (0, 1):
                            with sy.If_eq(r, slot):
                                sbx = (s_bx0, s_bx1)[slot]
                                sy.dma_start(out=bxsl[:, slot * ICH * 128:(slot + 1) * ICH * 128],
                                             in_=xlt[ds(t, 1)].squeeze(0)).then_inc(sbx, 16)

        def bi_scalar2(sc):
            sc.preload_act_table(AF.Silu)
            with sc.Fori(0, NFG) as fg:
                with sc.Fori(0, NT) as t:
                    n = fg * NT + t
                    sc.wait_ge(s_bmm, 2 * n + 1)
                    with sc.If(n >= 2):
                        sc.wait_ge(s_btp, n - 1)
                    with sc.register("bsc2") as r:
                        sc.reg_alu(r, sc.to_reg(t), 1, ALU.bitwise_and)
                        for slot in (0, 1):
                            with sc.If_eq(r, slot):
                                sc.activation(sgb[:, slot * FGS:(slot + 1) * FGS], bank(slot * 2)[:, 0:FGS], AF.Silu).then_inc(s_bsg, 1)
                    sc.wait_ge(s_btv, n + 1)
                    sc.dma_start(out=hts[ds(t, 1)].squeeze(0)[:, ds(fg * FPG * 128, FPG * 128)],
                                 in_=hstg[:, :]).then_inc(s_bhw, 16)
            sc.wait_ge(s_bhw, 16 * NFG * NT)

        def bi_tensor(te):
            with te.Fori(0, NFG) as fg:
                with te.register("btf") as rf:
                    te.reg_alu(rf, te.to_reg(fg), 1, ALU.bitwise_and)
                    with te.Fori(0, NT) as t:
                        n = fg * NT + t
                        with te.register("btt") as rt:
                            te.reg_alu(rt, te.to_reg(t), 1, ALU.bitwise_and)
                            for fslot in (0, 1):
                                with te.If_eq(rf, fslot):
                                    sgu = (s_gu0, s_gu1)[fslot]
                                    te.wait_ge(sgu, 32 * (fg // 2 + 1))
                                    for tslot in (0, 1):
                                        with te.If_eq(rt, tslot):
                                            sbx = (s_bx0, s_bx1)[tslot]
                                            te.wait_ge(sbx, 16 * (fg * (NT // 2) + t // 2 + 1))
                                            with te.If(n >= 2):
                                                te.wait_ge(s_bdv, n - 1)
                                            for mat in range(2):
                                                last = None
                                                for ich in range(ICH):
                                                    last = te.matmul(
                                                        bank(tslot * 2 + mat)[:, 0:FGS],
                                                        bxsl[:, (tslot * ICH + ich) * 128:(tslot * ICH + ich) * 128 + 128],
                                                        gusl[:, ((fslot * 2 + mat) * ICH + ich) * FGS:((fslot * 2 + mat) * ICH + ich) * FGS + FGS],
                                                        start=(ich == 0), stop=(ich == ICH - 1))
                                                last.then_inc(s_bmm, 1)
                        te.wait_ge(s_bdv, n + 1)
                        with te.If(n >= 1):
                            te.wait_ge(s_btv, n)
                        with te.register("btt2") as rt2:
                            te.reg_alu(rt2, te.to_reg(t), 1, ALU.bitwise_and)
                            for tslot in (0, 1):
                                with te.If_eq(rt2, tslot):
                                    last = None
                                    for k in range(FPG):
                                        last = te.transpose(ptr[:, k * 128:(k + 1) * 128],
                                                            sgb[:, tslot * FGS + k * 128:tslot * FGS + (k + 1) * 128],
                                                            ident[:, :])
                                    last.then_inc(s_btp, 1)

        def bi_vector(v):
            with v.Fori(0, NFG) as fg:
                with v.Fori(0, NT) as t:
                    n = fg * NT + t
                    v.wait_ge(s_bsg, n + 1)
                    v.wait_ge(s_bmm, 2 * n + 2)
                    with v.register("bvt") as rt:
                        v.reg_alu(rt, v.to_reg(t), 1, ALU.bitwise_and)
                        for tslot in (0, 1):
                            with v.If_eq(rt, tslot):
                                v.tensor_tensor(sgb[:, tslot * FGS:(tslot + 1) * FGS],
                                                sgb[:, tslot * FGS:(tslot + 1) * FGS],
                                                bank(tslot * 2 + 1)[:, 0:FGS], ALU.mult).then_inc(s_bdv, 1)
                    v.wait_ge(s_btp, n + 1)
                    with v.If(n >= 1):
                        v.wait_ge(s_bhw, 16 * n)
                    last = None
                    for k in range(FPG):
                        last = v.tensor_copy(hstg[:, k * 128:(k + 1) * 128], ptr[:, k * 128:(k + 1) * 128])
                    last.then_inc(s_btv, 1)

        def bi_gpsimd(g):
            with g.Fori(0, NFG) as fg:
                with g.If(fg >= 2):
                    g.wait_ge(s_bmm, 2 * (fg - 1) * NT)
                with g.register("gfg") as r:
                    g.reg_alu(r, g.to_reg(fg), 1, ALU.bitwise_and)
                    for slot in (0, 1):
                        with g.If_eq(r, slot):
                            sgu = (s_gu0, s_gu1)[slot]
                            for mat in range(2):
                                g.dma_start(out=gusl[:, (slot * 2 + mat) * ICH * FGS:(slot * 2 + mat + 1) * ICH * FGS],
                                            in_=gu[mat:mat + 1].squeeze(0)[ds(fg, 1)].squeeze(0).rearrange("p i f -> p (i f)")).then_inc(sgu, 16)

        DQB = [0, FCH // 3, 2 * FCH // 3, FCH]

        def d_slabq(eng, og, qi):
            f0, f1 = DQB[qi], DQB[qi + 1]
            with eng.If(og >= 1):
                eng.wait_ge(s_dmm, og * NT)
            eng.dma_start(out=dsl[:, f0 * 512:f1 * 512],
                          in_=dwt_b[ds(og, 1)].squeeze(0)[:, f0 * 512:f1 * 512]).then_inc((s_dl, s_dl2, s_dl3)[qi], 16)

        def d_gpsimd(g):
            with g.Fori(0, OG) as og:
                d_slabq(g, og, 0)
                with g.Fori(0, NT) as t:
                    n = og * NT + t
                    g.wait_ge(s_dcp, n + 1)
                    with g.register("gdt") as r:
                        g.reg_alu(r, g.to_reg(t), 1, ALU.bitwise_and)
                        for slot in (0, 1):
                            with g.If_eq(r, slot):
                                g.dma_start(out=xs[ds(t * 128, 128), ts(og, 512)],
                                            in_=dstg[:, slot * 512:(slot + 1) * 512],
                                            accum_op=ALU.add).then_inc(s_dac, 16)
            g.wait_ge(s_dac, 16 * OG * NT)

        def d_sync(sy):
            with sy.Fori(0, OG) as og:
                d_slabq(sy, og, 1)
                with sy.Fori(0, NT) as t:
                    n = og * NT + t
                    with sy.If(n >= 2):
                        sy.wait_ge(s_dmm, n - 1)
                    with sy.register("dsy") as r:
                        sy.reg_alu(r, sy.to_reg(t), 1, ALU.bitwise_and)
                        for slot in (0, 1):
                            with sy.If_eq(r, slot):
                                shx = (s_hx0, s_hx1)[slot]
                                sy.dma_start(out=hsl[:, slot * FCH * 128:(slot + 1) * FCH * 128],
                                             in_=hts[ds(t, 1)].squeeze(0)).then_inc(shx, 16)

        def d_scalar(sc):
            with sc.Fori(0, OG) as og:
                d_slabq(sc, og, 2)
                with sc.Fori(0, NT) as t:
                    n = og * NT + t
                    sc.wait_ge(s_dmm, n + 1)
                    with sc.If(n >= 2):
                        sc.wait_ge(s_dac, 16 * n)
                    with sc.register("dsc2") as r:
                        sc.reg_alu(r, sc.to_reg(t), 1, ALU.bitwise_and)
                        for slot in (0, 1):
                            with sc.If_eq(r, slot):
                                sc.activation(dstg[:, slot * 512:(slot + 1) * 512], bank(slot), AF.Copy).then_inc(s_dcp, 1)

        def d_tensor(te):
            with te.Fori(0, OG) as og:
                te.wait_ge(s_dl, 16 * (og + 1))
                te.wait_ge(s_dl2, 16 * (og + 1))
                te.wait_ge(s_dl3, 16 * (og + 1))
                with te.Fori(0, NT) as t:
                    n = og * NT + t
                    with te.register("btd") as rt:
                        te.reg_alu(rt, te.to_reg(t), 1, ALU.bitwise_and)
                        for tslot in (0, 1):
                            with te.If_eq(rt, tslot):
                                shx = (s_hx0, s_hx1)[tslot]
                                te.wait_ge(shx, 16 * (og * (NT // 2) + t // 2 + 1))
                                with te.If(n >= 2):
                                    te.wait_ge(s_dcp, n - 1)
                                last = None
                                for f in range(FCH):
                                    last = te.matmul(bank(tslot),
                                                     hsl[:, (tslot * FCH + f) * 128:(tslot * FCH + f) * 128 + 128],
                                                     dsl[:, f * 512:(f + 1) * 512],
                                                     start=(f == 0), stop=(f == FCH - 1))
                                last.then_inc(s_dmm, 1)

        def mk_qphase(si, st):
            return dict(
                gpsimd=lambda x: (ln_xload(x, si, wg=lambda g, t: qg_weave(g, st, t)),
                                  q_gpsimd(x, st, woven_og0=True)),
                vector=lambda x: (ln_v(x, si, wv=lambda v, t: qv_weave(v, st, t)),
                                  q_vector(x, st, woven_og0=True)),
                scalar=lambda x: (ln_s(x, si, ws=lambda c2, t: qs_weave(c2, st, t)),
                                  q_scalar(x, st, woven_og0=True)),
                tensor=lambda x: (ln_t(x, si, weave=lambda te, t: q_weave(te, st, t)),
                                  q_tensor(x, st, woven_og0=True)),
                sync=lambda x: q_sync(x, st))

        PHASES = [
            dict(gpsimd=p0_g, vector=p0_v, scalar=p0_s),
            dict(gpsimd=dq_g, vector=dq_v, tensor=dq_t, scalar=dq_s),
            mk_qphase(0, 0),
            dict(gpsimd=lambda x: (ln_xload(x, 1), a_gpsimd(x)),
                 vector=lambda x: (ln_v(x, 1), a_vector(x)),
                 scalar=lambda x: (ln_s(x, 1), a_scalar(x)),
                 tensor=lambda x: (ln_t(x, 1), a_tensor(x)),
                 sync=a_sync),
            dict(gpsimd=bi_gpsimd, scalar=bi_scalar2, tensor=bi_tensor, vector=bi_vector, sync=bi_sync),
            dict(gpsimd=d_gpsimd, scalar=d_scalar, tensor=d_tensor, sync=d_sync),
            mk_qphase(2, 1),
        ]

        def run_phases(eng, name):
            phs = PHASES if nphases is None else PHASES[:nphases]
            for i, ph in enumerate(phs):
                if i > 0:
                    eng.wait_ge(s_bar, NENG * i)
                fn = ph.get(name)
                if fn:
                    fn(eng)
                eng.nop().then_inc(s_bar, 1)

        @block.gpsimd
        def _(g):
            run_phases(g, "gpsimd")
            if dbg:
                g.wait_ge(s_bar, NENG * len(PHASES if nphases is None else PHASES[:nphases]))
                g.dma_start(out=wdbg, in_=wsl).then_inc(s_p0, 16)
                g.dma_start(out=xdbg, in_=xsl).then_inc(s_p0, 16)
                g.wait_ge(s_p0, 10 * 16)
            g.wait_ge(s_bar, NENG * len(PHASES if nphases is None else PHASES[:nphases]))
            g.dma_start(out=y_out, in_=xs).then_inc(s_out, 16)
            g.wait_ge(s_out, 16)

        @block.vector
        def _(v):
            run_phases(v, "vector")

        @block.scalar
        def _(sc):
            run_phases(sc, "scalar")

        @block.tensor
        def _(te):
            run_phases(te, "tensor")

        @block.sync
        def _(sy):
            run_phases(sy, "sync")
            sy.wait_ge(s_out, 16)

    return nc


def prep_inputs(c, hidden_states, up_w, gate_w, local_point, lao, fuse_w,
                discount, ffn_gate_w, ffn_up_w, ffn_down_w, qweight, l4, l8, bias):
    D, U, F, G = c.D, c.U, c.F, c.G
    T = c.T
    x = np.ascontiguousarray(hidden_states.reshape(-1, D))
    assert x.shape[0] == NCORES * T

    upg = np.stack([up_w.T, gate_w.T]).reshape(2, c.ICH, 128, U)
    upg = np.ascontiguousarray(upg.transpose(0, 2, 1, 3).reshape(2, 128, c.ICH * U).astype(np.float32))
    Wms = [lao] + [fuse_w[:, :, m] for m in range(G)]
    fz = np.empty((1 + G, c.OG, 128, c.UCH, 512), np.float32)
    for m, W in enumerate(Wms):
        wt = W.T.reshape(c.UCH, 128, c.OG, 512)
        fz[m] = wt.transpose(2, 1, 0, 3)
    gz = np.empty((2, c.NFG, 128, c.ICH, c.FGS), np.float32)
    for mat, W in enumerate([ffn_gate_w, ffn_up_w]):
        wt = W.T.reshape(c.ICH, 128, c.NFG, c.FGS)
        gz[mat] = wt.transpose(2, 1, 0, 3)
    dt_ = ffn_down_w.T.reshape(c.FCH, 128, c.OG, 512)
    dz = np.ascontiguousarray(dt_.transpose(2, 1, 0, 3))
    qz, l4z, l8z = [], [], []
    for k in range(NCORES):
        sl = slice(k * c.OSH, (k + 1) * c.OSH)
        qz.append(np.ascontiguousarray(
            qweight[:, sl, :].transpose(1, 0, 2).reshape(c.OT, 128, 8, c.QW)))
        l4z.append(np.ascontiguousarray(l4[sl].reshape(c.OT, 128, c.NCL)))
        l8z.append(np.ascontiguousarray(l8[sl].reshape(c.OT, 128, c.NCH)))
    identc = np.eye(128, dtype=np.float32).astype(bass_bf16())
    clsv = np.broadcast_to(np.arange(c.NCH, dtype=np.float32), (128, c.NCH)).copy()
    shv = np.broadcast_to((np.uint32(1) << np.arange(32, dtype=np.uint32)).view(np.int32), (128, 32)).copy()
    lpr = np.broadcast_to(local_point.astype(np.float32), (128, U)).copy()
    dscr = np.broadcast_to(discount.astype(np.float32), (128, G)).copy()
    bzr = np.broadcast_to(bias.astype(np.float32), (128, D)).copy()
    cnv = np.broadcast_to(np.array([1e-5, 1.0 / D, -1.0, 0.0], np.float32), (128, 4)).copy()

    maps = []
    for k in range(NCORES):
        maps.append(dict(
            x_in=np.ascontiguousarray(x[k * T:(k + 1) * T]),
            upgate=upg, fuse=fz, gu=gz, dwt=dz,
            qwt=qz[k], lut4=l4z[k], lut8=l8z[k],
            identc=identc, clsvec=clsv, shiftv=shv,
            lp_rep=lpr, disc_rep=dscr, bias_rep=bzr, constv=cnv,
        ))
    return maps


def bass_bf16():
    import ml_dtypes
    return ml_dtypes.bfloat16


def kernel(L0_attn_hidden_states, L0_attn_up_w, L0_attn_gate_w,
           L0_attn_local_point, L0_attn_local_approx_output, L0_attn_fuse_weight,
           L0_attn_discount_factor, L0_attn_ffn_gate_w, L0_attn_ffn_up_w,
           L0_attn_ffn_down_w, L0_ffn_qweight, L0_ffn_lut4, L0_ffn_lut8, L0_ffn_bias):
    c = REAL
    maps = prep_inputs(
        c,
        np.asarray(L0_attn_hidden_states, np.float32),
        np.asarray(L0_attn_up_w, np.float32),
        np.asarray(L0_attn_gate_w, np.float32),
        np.asarray(L0_attn_local_point, np.float32),
        np.asarray(L0_attn_local_approx_output, np.float32),
        np.asarray(L0_attn_fuse_weight, np.float32),
        np.asarray(L0_attn_discount_factor, np.float32),
        np.asarray(L0_attn_ffn_gate_w, np.float32),
        np.asarray(L0_attn_ffn_up_w, np.float32),
        np.asarray(L0_attn_ffn_down_w, np.float32),
        np.asarray(L0_ffn_qweight, np.int32),
        np.asarray(L0_ffn_lut4, np.float32),
        np.asarray(L0_ffn_lut8, np.float32),
        np.asarray(L0_ffn_bias, np.float32),
    )
    nc = build(c)
    nc.finalize()
    res = run_bass_kernel_spmd(nc, maps, core_ids=list(range(NCORES)))
    out = np.concatenate([r["y_out"] for r in res.results], axis=0)
    return out.reshape(4, 4096, 4096)


# version 13
# speedup vs baseline: 1.0273x; 1.0273x over previous
import numpy as np
import concourse.bass as bass
import concourse.bacc as bacc
from concourse.bass import ts, ds
from concourse.expressions_rust import s_assert_within
import concourse.mybir as mybir
from concourse.bass_utils import run_bass_kernel_spmd

F32 = mybir.dt.float32
BF16 = mybir.dt.bfloat16
I32 = mybir.dt.int32
ALU = mybir.AluOpType
AF = mybir.ActivationFunctionType
AX = mybir.AxisListType

NCORES = 8
LN_EPS = 1e-5
NENG = 5


class Cfg:
    def __init__(self, D=4096, U=1024, F=9984, FGS=384, T=2048, SUP=2, G=6):
        self.D, self.U, self.F, self.FGS, self.T, self.G = D, U, F, FGS, T, G
        self.NCL, self.NCH = 16, 256
        self.NT = T // 128
        self.SUP = SUP
        self.NSUP = self.NT // SUP
        self.ICH = D // 128
        self.OG = D // 512
        self.UCH = U // 128
        self.UG = U // 512
        self.NFG = F // FGS
        self.FCH = F // 128
        self.FPG = FGS // 128
        self.OSH = D // NCORES
        self.OT = self.OSH // 128
        self.QW = D // 32
        self.RPOG = 512 // self.OSH if self.OSH < 512 else 1
        self.THB = float(2 * np.spacing(np.float32(D)))
        assert D % 512 == 0 and U % 512 == 0 and F % FGS == 0 and FGS % 128 == 0
        assert T % 128 == 0 and self.NT % SUP == 0 and self.OSH % 128 == 0
        assert self.NT % 2 == 0 and self.OG % 2 == 0 and self.NFG % 2 == 0
        assert self.SUP * self.UG <= 6 and self.ICH % 2 == 0


REAL = Cfg()


def build(c, nphases=None, dbg=False, dbg_noselect=False):
    NT, OG, ICH, UCH, UG = c.NT, c.OG, c.ICH, c.UCH, c.UG
    NFG, FCH, FPG, FGS = c.NFG, c.FCH, c.FPG, c.FGS
    OT, NSUP, SUP, G, D, U, QW = c.OT, c.NSUP, c.SUP, c.G, c.D, c.U, c.QW
    NCL, NCH, RPOG, OSH = c.NCL, c.NCH, c.RPOG, c.OSH
    NTOT = OG * NT
    TRND = 16
    NRND = (ICH + TRND - 1) // TRND
    NTT = SUP * UCH
    NRND_A = (NTT + TRND - 1) // TRND

    nc = bacc.Bacc("TRN2", target_bir_lowering=False)

    x_in = nc.dram_tensor("x_in", [c.T, D], F32, kind="ExternalInput").ap()
    y_out = nc.dram_tensor("y_out", [c.T, D], F32, kind="ExternalOutput").ap()
    upgate = nc.dram_tensor("upgate", [2, 128, ICH * U], F32, kind="ExternalInput").ap()
    fuse = nc.dram_tensor("fuse", [1 + G, OG, 128, UCH, 512], F32, kind="ExternalInput").ap()
    gu = nc.dram_tensor("gu", [2, NFG, 128, ICH, FGS], F32, kind="ExternalInput").ap()
    dwt = nc.dram_tensor("dwt", [OG, 128, FCH, 512], F32, kind="ExternalInput").ap()
    qwt = nc.dram_tensor("qwt", [OT, 128, 8, QW], I32, kind="ExternalInput").ap()
    lut4 = nc.dram_tensor("lut4", [OT, 128, NCL], F32, kind="ExternalInput").ap()
    lut8 = nc.dram_tensor("lut8", [OT, 128, NCH], F32, kind="ExternalInput").ap()
    identc = nc.dram_tensor("identc", [128, 128], BF16, kind="ExternalInput").ap()
    clsvec = nc.dram_tensor("clsvec", [128, NCH], F32, kind="ExternalInput").ap()
    shiftv = nc.dram_tensor("shiftv", [128, 32], I32, kind="ExternalInput").ap()
    lp_rep = nc.dram_tensor("lp_rep", [128, U], F32, kind="ExternalInput").ap()
    disc_rep = nc.dram_tensor("disc_rep", [128, G], F32, kind="ExternalInput").ap()
    bias_rep = nc.dram_tensor("bias_rep", [128, D], F32, kind="ExternalInput").ap()
    constv = nc.dram_tensor("constv", [128, 4], F32, kind="ExternalInput").ap()

    xs = nc.dram_tensor("xs", [c.T, D], F32, kind="Internal").ap()
    b05d = nc.dram_tensor("b05d", [128, D], F32, kind="Internal").ap()
    fuse_b = nc.dram_tensor("fuse_b", [1 + G, OG, 128, UCH * 512], BF16, kind="Internal").ap()
    upg_b = nc.dram_tensor("upg_b", [2, 128, ICH * U], BF16, kind="Internal").ap()
    dwt_b = nc.dram_tensor("dwt_b", [OG, 128, FCH * 512], BF16, kind="Internal").ap()
    xlt = nc.dram_tensor("xlt", [NT, 128, ICH * 128], BF16, kind="Internal").ap()
    hts = nc.dram_tensor("hts", [NT, 128, FCH * 128], BF16, kind="Internal").ap()
    cc_in = nc.dram_tensor("cc_in", [2 * OT * 128, D], BF16, kind="Internal").ap()
    wdbg = xdbg = None
    if dbg:
        wdbg = nc.dram_tensor("wdbg", [128, 2 * 2 * ICH * 512], BF16, kind="ExternalOutput").ap()
        xdbg = nc.dram_tensor("xdbg", [128, 2 * ICH * 128], BF16, kind="ExternalOutput").ap()
    cc_out = nc.dram_tensor("cc_out", [NCORES * 2 * OT * 128, D], BF16, kind="Internal").ap()

    from contextlib import ExitStack
    with ExitStack() as ctx:
        e = ctx.enter_context

        AR1 = e(nc.sbuf_tensor("AR1", [128, 65536], BF16))
        AR2 = e(nc.sbuf_tensor("AR2", [128, 24576], BF16))

        def v1(off, n, dt=BF16):
            if dt == BF16:
                return AR1[:, off // 2:off // 2 + n]
            return AR1[:, off // 2:off // 2 + n * 2].bitcast(dt)

        def v2(off, n, dt=BF16):
            if dt == BF16:
                return AR2[:, off // 2:off // 2 + n]
            return AR2[:, off // 2:off // 2 + n * 2].bitcast(dt)

        ident = e(nc.sbuf_tensor("ident", [128, 128], BF16))
        maskb = e(nc.sbuf_tensor("maskb", [128, NT], F32))
        lpb = e(nc.sbuf_tensor("lpb", [128, U], F32))
        expd = e(nc.sbuf_tensor("expd", [128, G], F32))
        clsb = e(nc.sbuf_tensor("clsb", [128, NCH], F32))
        shb = e(nc.sbuf_tensor("shb", [128, 32], I32))
        red = e(nc.sbuf_tensor("red", [128, 2 * ICH + 16], F32))
        sgb = e(nc.sbuf_tensor("sgb", [128, 2 * FGS], BF16))
        hstg = e(nc.sbuf_tensor("hstg", [128, FPG * 128], BF16))
        lutb = e(nc.sbuf_tensor("lutb", [128, NCL + NCH], F32))
        cns = e(nc.sbuf_tensor("cns", [128, 4], F32))

        qwb = v1(0, 8 * QW, I32)
        bitb = v1(8 * QW * 4, D, I32)
        bitf = v1(8 * QW * 4 + 4 * D, D, F32)
        idx8f = v1(8 * QW * 4 + 8 * D, D, F32)
        idx4f = v1(8 * QW * 4 + 12 * D, D, F32)
        idx4b = v1(8 * QW * 4 + 16 * D, D)
        idx8b = v1(8 * QW * 4 + 18 * D, D)
        wacc = v1(8 * QW * 4 + 20 * D, D)
        wtmp = v1(8 * QW * 4 + 22 * D, D)
        waccd = [[v1((28 + 2 * h + a) * D, D // 2) for a in range(2)] for h in range(2)]

        xt2 = v1(0, 2 * D, F32)
        dti = v1(8 * D, D, F32)
        sqt = v1(12 * D, D, F32)

        wsl = v1(0, 2 * 2 * ICH * 512)

        deltab = v1(0, SUP * U, F32)
        pband = v1(4 * SUP * U, SUP * U, F32)
        ub = v1(8 * SUP * U, SUP * U)
        keepb = v1(10 * SUP * U, SUP * U)
        tmpm = v1(12 * SUP * U, SUP * U)
        tmpT = v1(14 * SUP * U, SUP * UCH * 128)
        abst = v1(16 * SUP * U, U, F32)
        fsl = v1(16 * SUP * U + 4 * U, 2 * UCH * 512)
        ugslab = v1(max(16 * SUP * U + 4 * U + 4 * UCH * 512, 16 * D), ICH * U)

        gusl = v1(0, 2 * 2 * ICH * FGS)
        dsl = v1(0, FCH * 512)

        xlb = v2(0, D)
        xstg = v2(2 * D, ICH * 128)

        xsl = v2(4 * D, 2 * ICH * 128)
        bias05v = v2(8 * D, 2 * 512, F32)
        ylb = v2(8 * D + 4096, 2 * 512, F32)
        ydb = v2(8 * D + 8192, 2 * 512, F32)
        ystg = v2(8 * D + 12288, 2 * 512, F32)
        b05s = v2(0, D, F32)
        tmpv = [v2(i * D, D // 2) for i in range(4)]
        prodv = [v2(4 * D + i * D, D // 2) for i in range(8)]

        axsl = v2(0, SUP * ICH * 128)
        ugtile = v2(2 * SUP * ICH * 128, 2 * U)
        astg_a = v2(2 * SUP * ICH * 128 + 2 * 2 * U, SUP * 512, F32)
        astg_b = v2(2 * SUP * ICH * 128 + 2 * 2 * U + SUP * 2048, SUP * 512, F32)
        astg2 = [astg_a, astg_b]

        bxsl = v2(0, 2 * ICH * 128)

        hsl = v2(0, 2 * FCH * 128)
        dstg = v2(2 * 2 * FCH * 128 + 2048, 2 * 512, F32)

        pbk = [e(nc.psum_tensor(f"pb{i}", [128, 512], F32)) for i in range(6)]
        ptrT = e(nc.psum_tensor("ptrT", [128, 2048], BF16))
        ptr = ptrT[:, :]

        def bank(i):
            return pbk[i][:, :]

        def sem(n):
            return e(nc.semaphore(n))

        s_bar = sem("s_bar")
        s_p0 = sem("s_p0"); s_p0c = sem("s_p0c")
        s_dq = sem("s_dq"); s_dv = sem("s_dv"); s_dw = sem("s_dw"); s_cc = sem("s_cc"); s_pc = sem("s_pc")
        s_ds = sem("s_ds"); s_dp = sem("s_dp"); s_de = sem("s_de")
        s_do = sem("s_do"); s_dmg = sem("s_dmg")
        s_lx0 = sem("s_lx0"); s_lx1 = sem("s_lx1")
        s_lc = sem("s_lc"); s_la = sem("s_la")
        s_ltp = sem("s_ltp"); s_lev = sem("s_lev"); s_lw = sem("s_lw")
        s_qw0 = sem("s_qw0"); s_qw1 = sem("s_qw1")
        s_qx0 = sem("s_qx0"); s_qx1 = sem("s_qx1")
        s_qmm = sem("s_qmm"); s_qyl = sem("s_qyl"); s_qdv = sem("s_qdv")
        s_qev = sem("s_qev"); s_qac = sem("s_qac"); s_qbs = sem("s_qbs")
        s_ug0 = sem("s_ug0"); s_ug1 = sem("s_ug1"); s_ax = sem("s_ax")
        s_amm = sem("s_amm"); s_aev = sem("s_aev"); s_adv = sem("s_adv")
        s_atp = sem("s_atp"); s_atv = sem("s_atv")
        s_af0 = sem("s_af0"); s_af1 = sem("s_af1")
        s_apm = sem("s_apm"); s_apy = sem("s_apy"); s_aac0 = sem("s_aac0"); s_aac1 = sem("s_aac1")
        s_gu0 = sem("s_gu0"); s_gu1 = sem("s_gu1")
        s_bx0 = sem("s_bx0"); s_bx1 = sem("s_bx1")
        s_bmm = sem("s_bmm"); s_bsg = sem("s_bsg"); s_bdv = sem("s_bdv")
        s_btp = sem("s_btp"); s_btv = sem("s_btv"); s_bhw = sem("s_bhw")
        s_dl = sem("s_dl"); s_dl2 = sem("s_dl2"); s_dl3 = sem("s_dl3"); s_hx0 = sem("s_hx0"); s_hx1 = sem("s_hx1")
        s_dmm = sem("s_dmm"); s_dcp = sem("s_dcp"); s_dac = sem("s_dac")
        s_out = sem("s_out")

        block = e(nc.Block())

        KD_M, KD_N = 17, 13
        KA_M, KA_N = 4, 3
        LN_MASK = [True, False, True]

        def lkd(i):
            return KD_M if LN_MASK[i] else KD_N

        def lka(i):
            return KA_M if LN_MASK[i] else KA_N

        LC_B = [sum(NT * lkd(j) for j in range(i)) for i in range(4)]
        LA_B = [sum(NT * lka(j) for j in range(i)) for i in range(4)]
        LX_B = [i * (NT // 2) for i in range(4)]
        LT_B = [i * NT * NRND for i in range(4)]
        LW_B = [i * NT for i in range(4)]

        QWSLAB = 2 * ICH * RPOG
        QX_B = [st * OG * (NT // 2) for st in range(2)]
        QWS_B = [st * (OG // 2) * QWSLAB for st in range(2)]
        QG_B = [st * NTOT for st in range(2)]
        QB_B = [st * OG for st in range(2)]

        DVX = 2 + 8 * 34 + 4 + 2
        HD = D // 2
        HCH = HD // 512
        PC = 216
        NOWN = NCH - PC
        OWNH = 2 + 2 * (NOWN - 2)
        OWNOT = 2 * OWNH
        MRG = 4
        SELOT = 2 * (NCL + PC)
        EVOT = 4 * HCH

        AMM_S = 2 * ICH
        AEV_S = 2 * SUP * UG
        UGLD_S = 2 * ICH

        def adv_m(m):
            if m == 0:
                return 0
            ops = 2
            if m >= 2:
                ops += 1
            if m > 4:
                ops += 1
            return ops * SUP

        ADV_S = SUP * UG + 3 * SUP + sum(adv_m(m) for m in range(1 + G))
        ADV_END = []
        _a = SUP * UG + 3 * SUP
        for m in range(1 + G):
            _a += adv_m(m)
            ADV_END.append(_a)

        def p0_g(g):
            g.dma_start(out=ident[:, :], in_=identc).then_inc(s_p0, 16)
            g.dma_start(out=clsb[:, :], in_=clsvec).then_inc(s_p0, 16)
            g.dma_start(out=shb[:, :], in_=shiftv).then_inc(s_p0, 16)
            g.dma_start(out=lpb[:, :], in_=lp_rep).then_inc(s_p0, 16)
            g.dma_start(out=b05s, in_=bias_rep).then_inc(s_p0, 16)
            g.dma_start(out=expd[:, :], in_=disc_rep).then_inc(s_p0, 16)
            g.dma_start(out=cns[:, :], in_=constv).then_inc(s_p0, 16)
            g.dma_start(out=xs, in_=x_in).then_inc(s_p0, 16)
            g.wait_ge(s_p0, 8 * 16)
            g.wait_ge(s_p0c, 2)
            g.dma_start(out=b05d, in_=b05s).then_inc(s_p0, 16)
            g.wait_ge(s_p0, 9 * 16)

        def p0_v(v):
            v.wait_ge(s_p0, 8 * 16)
            v.tensor_scalar(b05s, b05s, 0.5, None, ALU.mult).then_inc(s_p0c, 1)

        def p0_s(sc):
            sc.preload_act_table(AF.Exp)
            sc.wait_ge(s_p0, 8 * 16)
            sc.activation(expd[:, :], expd[:, :], AF.Exp, scale=cns[:, 2:3]).then_inc(s_p0c, 1)

        def dq_g(g):
            for m in range(1 + G):
                g.dma_start(out=fuse_b[m:m + 1].squeeze(0).rearrange("o p q -> (o p) q"),
                            in_=fuse[m:m + 1].squeeze(0).rearrange("o p u q -> (o p) (u q)")).then_inc(s_pc, 16)
            for mat in range(2):
                g.dma_start(out=upg_b[mat:mat + 1].squeeze(0),
                            in_=upgate[mat:mat + 1, :, :].squeeze(0)).then_inc(s_pc, 16)
            with g.Fori(0, OG) as ogq:
                g.dma_start(out=dwt_b[ds(ogq, 1)].squeeze(0),
                            in_=dwt[ds(ogq, 1)].squeeze(0).rearrange("p f q -> p (f q)")).then_inc(s_pc, 16)
            with g.Fori(0, OT) as ot:
                with g.If(ot >= 1):
                    g.wait_ge(s_dv, ot * DVX)
                    g.wait_ge(s_ds, ot * SELOT)
                    g.wait_ge(s_do, ot * OWNOT)
                g.dma_start(out=qwb, in_=qwt[ds(ot, 1)].squeeze(0).rearrange("p a b -> p (a b)")).then_inc(s_dq, 16)
                g.dma_start(out=lutb[:, 0:NCL], in_=lut4[ds(ot, 1)].squeeze(0)).then_inc(s_dq, 16)
                g.dma_start(out=lutb[:, NCL:NCL + NCH], in_=lut8[ds(ot, 1)].squeeze(0)).then_inc(s_dq, 16)
                g.wait_ge(s_de, EVOT * (ot + 1))
                g.wait_ge(s_dmg, MRG * (ot + 1))
                g.dma_start(out=cc_in[ds(ot * 128, 128), :], in_=wacc).then_inc(s_dw, 16)
                g.dma_start(out=cc_in[ds((OT + ot) * 128, 128), :], in_=wtmp).then_inc(s_dw, 16)
            g.wait_ge(s_dw, 32 * OT)
            g.collective_compute(
                "AllGather", ALU.bypass,
                replica_groups=[list(range(NCORES))],
                ins=[cc_in], outs=[cc_out],
            ).then_inc(s_cc)
            g.wait_ge(s_cc, 1)
            g.wait_ge(s_pc, 16 * (3 + G + OG))

        def dq_v(v):
            bitb3 = bitb.rearrange("p (w s) -> p w s", s=32)
            with v.Fori(0, OT) as ot:
                bb = ot * DVX
                v.wait_ge(s_dq, 48 * (ot + 1))
                j = [0]

                def step(inst):
                    inst.then_inc(s_dv, 1)
                    j[0] += 1

                def wj():
                    v.wait_ge(s_dv, bb + j[0])

                step(v.memset(idx8f, 0.0))
                step(v.memset(idx4f, 0.0))
                for p in range(8):
                    npre = 2 + p * 34 + (min(p, 4))
                    with v.Fori(0, 32) as s:
                        v.wait_ge(s_dv, bb + npre + s)
                        v.tensor_scalar(bitb3[:, :, ds(s, 1)].squeeze(2),
                                        qwb[:, p * QW:(p + 1) * QW],
                                        shb[:, ds(s, 1)], None,
                                        ALU.bitwise_and).then_inc(s_dv, 1)
                    j[0] += 32
                    wj()
                    step(v.tensor_scalar(bitf, bitb, 0, None, ALU.not_equal))
                    wj()
                    step(v.scalar_tensor_tensor(idx8f, bitf, float(2 ** p), idx8f, ALU.mult, ALU.add))
                    wj()
                    if p < 4:
                        step(v.scalar_tensor_tensor(idx4f, bitf, float(2 ** p), idx4f, ALU.mult, ALU.add))
                        wj()
                step(v.tensor_copy(idx4b, idx4f))
                step(v.tensor_copy(idx8b, idx8f))
                assert j[0] == DVX, (j[0], DVX)
                wj()
                bsel = ot * SELOT
                bown = ot * OWNOT

                for h in (0, 1):
                    hb = bsel + h * (NCL + PC)
                    ob = bown + h * OWNH
                    idx4h = idx4b[:, h * HD:(h + 1) * HD]
                    idx8h = idx8b[:, h * HD:(h + 1) * HD]

                    seq = [("d", PC), ("d", PC + 1)]
                    for k in range(PC + 2, NCH):
                        seq.append(("s", k))
                        if k >= PC + 3:
                            seq.append(("a", k - 1))
                    seq.append(("a", NCH - 1))
                    assert len(seq) == OWNH
                    pos = {op: i for i, op in enumerate(seq)}
                    own_i = [0]

                    def emit_own():
                        if own_i[0] >= len(seq):
                            return
                        typ, k = seq[own_i[0]]
                        own_i[0] += 1
                        a = k & 1
                        ccol = clsb[:, k:k + 1]
                        lcol = lutb[:, NCL + k:NCL + k + 1]
                        if typ == "d":
                            with v.If(ot >= 1):
                                v.wait_ge(s_dmg, (ot - 1) * MRG + 2 * (h + 1))
                            v.tensor_scalar(waccd[h][a], idx8h, ccol, lcol,
                                            ALU.is_equal, ALU.mult).then_inc(s_do, 1)
                        elif typ == "s":
                            dep = pos.get(("a", k - 4))
                            v.wait_ge(s_do, ob + (dep + 1 if dep is not None else 0))
                            v.tensor_scalar(tmpv[k & 3], idx8h, ccol, lcol,
                                            ALU.is_equal, ALU.mult).then_inc(s_do, 1)
                        else:
                            pk = ("a", k - 2) if ("a", k - 2) in pos else ("d", k - 2)
                            dep = max(pos[("s", k)], pos[pk])
                            v.wait_ge(s_do, ob + dep + 1)
                            v.tensor_tensor(waccd[h][a], waccd[h][a], tmpv[k & 3],
                                            ALU.add).then_inc(s_do, 1)

                    credit = [0.0]
                    ratio = OWNH / (NCL + PC)
                    for mi, (idxh_, l0, cnt) in enumerate(((idx4h, 0, NCL), (idx8h, NCL, PC))):
                        base = hb + (0 if mi == 0 else NCL)
                        for k in range(cnt):
                            n = base + k
                            off = h * (NCL + PC) + (0 if mi == 0 else NCL) + k
                            if off < 8:
                                with v.If(n >= 8):
                                    v.wait_ge(s_dp, n - 7)
                            else:
                                v.wait_ge(s_dp, n - 7)
                            v.tensor_scalar(prodv[k & 7], idxh_,
                                            clsb[:, k:k + 1], lutb[:, l0 + k:l0 + k + 1],
                                            ALU.is_equal, ALU.mult).then_inc(s_ds, 1)
                            credit[0] += ratio
                            while credit[0] >= 1.0:
                                emit_own()
                                credit[0] -= 1.0
                    while own_i[0] < len(seq):
                        emit_own()

                for h in (0, 1):
                    hs = slice(h * HD, (h + 1) * HD)
                    v.wait_ge(s_de, ot * EVOT + (2 * h + 2) * HCH)
                    v.tensor_tensor(wtmp[:, hs], wtmp[:, hs], waccd[h][0], ALU.add).then_inc(s_dmg, 1)
                    v.wait_ge(s_dmg, ot * MRG + 2 * h + 1)
                    v.tensor_tensor(wtmp[:, hs], wtmp[:, hs], waccd[h][1], ALU.add).then_inc(s_dmg, 1)

        def dq_t(te):
            with te.Fori(0, OT) as ot:
                bsel = ot * SELOT
                for h in (0, 1):
                    for mi, cnt in ((0, NCL), (1, PC)):
                        sec = h * 2 + mi
                        base = bsel + h * (NCL + PC) + (0 if mi == 0 else NCL)
                        gsec = ot * 4 + sec
                        if sec == 0:
                            with te.If(ot >= 1):
                                te.wait_ge(s_de, HCH * gsec)
                        else:
                            te.wait_ge(s_de, HCH * gsec)

                        def grp(k, first, last):
                            te.wait_ge(s_ds, base + k + 1)
                            lastmm = None
                            for j in range(HCH):
                                lastmm = te.matmul(bank(j), ident[:, :],
                                                   prodv[k & 7][:, j * 512:(j + 1) * 512],
                                                   start=first, stop=last)
                            lastmm.then_inc(s_dp, 1)

                        nch8 = cnt // 8
                        assert cnt % 8 == 0
                        for kk in range(8):
                            grp(kk, kk == 0, False)
                        if nch8 > 2:
                            with te.Fori(1, nch8 - 1) as c:
                                for kk in range(8):
                                    n = base + c * 8 + kk
                                    te.wait_ge(s_ds, n + 1)
                                    lastmm = None
                                    for j in range(HCH):
                                        lastmm = te.matmul(bank(j), ident[:, :],
                                                           prodv[kk][:, j * 512:(j + 1) * 512],
                                                           start=False, stop=False)
                                    lastmm.then_inc(s_dp, 1)
                        for kk in range(8):
                            k = (nch8 - 1) * 8 + kk
                            grp(k, False, kk == 7)

        def dq_s(sc):
            with sc.Fori(0, OT) as ot:
                with sc.If(ot >= 1):
                    sc.wait_ge(s_dw, 32 * ot)
                for h in (0, 1):
                    for mi, cnt in ((0, NCL), (1, PC)):
                        sec_end = ot * SELOT + h * (NCL + PC) + (NCL if mi == 0 else NCL + PC)
                        sc.wait_ge(s_dp, sec_end)
                        dst = (wacc, wtmp)[mi]
                        for j in range(HCH):
                            sc.activation(dst[:, h * HD + j * 512:h * HD + j * 512 + 512],
                                          bank(j), AF.Copy).then_inc(s_de, 1)

        def ln_xload(sy, si, wg=None):
            KD = lkd(si)
            with sy.Fori(0, NT) as t:
                with sy.If(t >= 2):
                    sy.wait_ge(s_lc, LC_B[si] + (t - 2) * KD + 4)
                with sy.register(f"lns{si}") as r:
                    sy.reg_alu(r, sy.to_reg(t), 1, ALU.bitwise_and)
                    with sy.If_eq(r, 0):
                        sy.dma_start(out=xt2[:, 0:D], in_=xs[ds(t * 128, 128), :]).then_inc(s_lx0, 16)
                    with sy.Else():
                        sy.dma_start(out=xt2[:, D:2 * D], in_=xs[ds(t * 128, 128), :]).then_inc(s_lx1, 16)
                if wg is not None:
                    wg(sy, t)

        def ln_v(v, si, wv=None):
            KD, KA = lkd(si), lka(si)
            mask = LN_MASK[si]
            with v.Fori(0, NT) as t:
                b = LC_B[si] + t * KD
                ba = LA_B[si] + t * KA
                with v.register(f"lnv{si}") as r:
                    v.reg_alu(r, v.to_reg(t), 1, ALU.bitwise_and)
                    for slot in (0, 1):
                        with v.If_eq(r, slot):
                            v.wait_ge((s_lx0, s_lx1)[slot], 16 * (LX_B[si] + t // 2 + 1))
                            x = xt2[:, slot * D:(slot + 1) * D]
                            v.tensor_reduce(red[:, 0:ICH], x.rearrange("p (a q) -> p a q", q=128), AX.X, ALU.add).then_inc(s_lc, 1)
                            v.wait_ge(s_lc, b + 1)
                            v.tensor_reduce(red[:, ICH:ICH + 1], red[:, 0:ICH], AX.X, ALU.add).then_inc(s_lc, 1)
                            v.wait_ge(s_lc, b + 2)
                            v.tensor_scalar(red[:, ICH + 1:ICH + 2], red[:, ICH:ICH + 1], 1.0 / D, None, ALU.mult).then_inc(s_lc, 1)
                            v.wait_ge(s_lc, b + 3)
                            v.tensor_scalar(dti, x, red[:, ICH + 1:ICH + 2], None, ALU.subtract).then_inc(s_lc, 1)
                            v.wait_ge(s_la, ba + 1)
                            v.tensor_reduce(red[:, 0:ICH], sqt.rearrange("p (a q) -> p a q", q=128), AX.X, ALU.add).then_inc(s_lc, 1)
                            v.wait_ge(s_lc, b + 5)
                            v.tensor_reduce(red[:, ICH + 2:ICH + 3], red[:, 0:ICH], AX.X, ALU.add).then_inc(s_lc, 1)
                            v.wait_ge(s_la, ba + 2)
                            v.reciprocal(red[:, ICH + 4:ICH + 5], red[:, ICH + 3:ICH + 4]).then_inc(s_lc, 1)
                            v.wait_ge(s_lc, b + 7)
                            v.tensor_scalar(red[:, ICH + 5:ICH + 6], red[:, ICH + 2:ICH + 3], 1.0 / D, LN_EPS, ALU.mult, ALU.add).then_inc(s_lc, 1)
                            v.wait_ge(s_la, ba + 3)
                            v.wait_ge(s_lc, b + 8)
                            v.tensor_tensor(red[:, ICH + 7:ICH + 8], red[:, ICH + 6:ICH + 7], red[:, ICH + 5:ICH + 6], ALU.mult).then_inc(s_lc, 1)
                            v.wait_ge(s_lc, b + 9)
                            v.tensor_scalar(red[:, ICH + 8:ICH + 9], red[:, ICH + 7:ICH + 8], -0.5, 1.5, ALU.mult, ALU.add).then_inc(s_lc, 1)
                            v.wait_ge(s_lc, b + 10)
                            v.tensor_tensor(red[:, ICH + 9:ICH + 10], red[:, ICH + 4:ICH + 5], red[:, ICH + 8:ICH + 9], ALU.mult).then_inc(s_lc, 1)
                            v.wait_ge(s_lc, b + 11)
                            v.tensor_scalar(sqt, dti, red[:, ICH + 9:ICH + 10], None, ALU.mult).then_inc(s_lc, 1)
                            v.wait_ge(s_lc, b + 12)
                            with v.If(t >= 1):
                                v.wait_ge(s_ltp, LT_B[si] + t * NRND)
                            v.tensor_copy(xlb, sqt).then_inc(s_lc, 1)
                            if mask:
                                v.wait_ge(s_la, ba + 4)
                                v.tensor_reduce(red[:, ICH:2 * ICH], dti.rearrange("p (a q) -> p a q", q=128), AX.X, ALU.add).then_inc(s_lc, 1)
                                v.wait_ge(s_lc, b + 14)
                                v.tensor_scalar(red[:, ICH:2 * ICH], red[:, ICH:2 * ICH], -128.0, None, ALU.add).then_inc(s_lc, 1)
                                v.wait_ge(s_lc, b + 15)
                                v.tensor_reduce(red[:, 2 * ICH:2 * ICH + 1], red[:, ICH:2 * ICH], AX.X, ALU.add).then_inc(s_lc, 1)
                                v.wait_ge(s_lc, b + 16)
                                v.tensor_scalar(maskb[:, ds(t, 1)], red[:, 2 * ICH:2 * ICH + 1], c.THB, None, ALU.is_gt).then_inc(s_lc, 1)
                for rnd in range(NRND):
                    ntile = min(TRND, ICH - rnd * TRND)
                    v.wait_ge(s_ltp, LT_B[si] + t * NRND + rnd + 1)
                    if rnd == 0:
                        with v.If(t >= 1):
                            v.wait_ge(s_lw, 16 * (LW_B[si] + t))
                    last = None
                    for k in range(ntile):
                        kk = rnd * TRND + k
                        last = v.tensor_copy(xstg[:, kk * 128:(kk + 1) * 128], ptr[:, k * 128:(k + 1) * 128])
                    last.then_inc(s_lev, 1)
                if wv is not None:
                    wv(v, t)

        def ln_s(sc, si, ws=None):
            KD, KA = lkd(si), lka(si)
            mask = LN_MASK[si]
            sc.preload_act_table(AF.Sqrt)
            with sc.Fori(0, NT) as t:
                b = LC_B[si] + t * KD
                sc.wait_ge(s_lc, b + 4)
                sc.activation(sqt, dti, AF.Square).then_inc(s_la, 1)
                sc.wait_ge(s_lc, b + 6)
                sc.activation(red[:, ICH + 3:ICH + 4], red[:, ICH + 2:ICH + 3], AF.Sqrt, bias=cns[:, 0:1], scale=cns[:, 1:2]).then_inc(s_la, 1)
                sc.wait_ge(s_lc, b + 7)
                sc.activation(red[:, ICH + 6:ICH + 7], red[:, ICH + 4:ICH + 5], AF.Square).then_inc(s_la, 1)
                if mask:
                    sc.wait_ge(s_lc, b + 12)
                    sc.activation(dti, sqt, AF.Square).then_inc(s_la, 1)
                sc.wait_ge(s_lev, LT_B[si] + (t + 1) * NRND)
                sc.dma_start(out=xlt[ds(t, 1)].squeeze(0), in_=xstg).then_inc(s_lw, 16)
                if ws is not None:
                    ws(sc, t)
            sc.wait_ge(s_lw, 16 * (LW_B[si] + NT))

        def ln_t(te, si, weave=None):
            KD = lkd(si)
            with te.Fori(0, NT) as t:
                b = LC_B[si] + t * KD
                te.wait_ge(s_lc, b + 13)
                for rnd in range(NRND):
                    ntile = min(TRND, ICH - rnd * TRND)
                    te.wait_ge(s_lev, LT_B[si] + t * NRND + rnd)
                    last = None
                    for k in range(ntile):
                        kk = rnd * TRND + k
                        last = te.transpose(ptr[:, k * 128:(k + 1) * 128], xlb[:, kk * 128:(kk + 1) * 128], ident[:, :])
                    last.then_inc(s_ltp, 1)
                if weave is not None:
                    weave(te, t)

        def q_sync(sy, st):
            si = 0 if st == 0 else 2
            lnd = LC_B[si] + NT * lkd(si)
            with sy.Fori(0, OG) as og:
                with sy.If(og >= 2):
                    sy.wait_ge(s_qev, QG_B[st] + (og - 1) * NT)
                with sy.register(f"qs{st}") as r:
                    sy.reg_alu(r, sy.to_reg(og), 1, ALU.bitwise_and)
                    for slot in (0, 1):
                        with sy.If_eq(r, slot):
                            if slot == 1:
                                sy.wait_ge(s_lc, lnd)
                            sy.dma_start(out=bias05v[:, slot * 512:(slot + 1) * 512],
                                         in_=b05d[:, ts(og, 512)]).then_inc(s_qbs, 16)
                            swl = (s_qw0, s_qw1)[slot]
                            for mat in range(2):
                                for ich in range(ICH):
                                    for rr in range(RPOG):
                                        src = cc_out[ds((og * RPOG + rr) * (2 * OT * 128) + mat * OT * 128, OT * 128),
                                                     ich * 128:(ich + 1) * 128]
                                        dof = (1 - slot) * 2 * ICH * 512 + (mat * ICH + ich) * 512 + rr * OSH
                                        sy.dma_start_transpose(wsl[:, dof:dof + OSH], src).then_inc(swl, 16)

        def q_xload(sc, st, og, t, n, tag):
            with sc.If(n >= 2):
                sc.wait_ge(s_qmm, QG_B[st] + n - 1)
            with sc.register(f"qc{st}{tag}") as r:
                sc.reg_alu(r, sc.to_reg(t), 1, ALU.bitwise_and)
                for slot in (0, 1):
                    with sc.If_eq(r, slot):
                        sxl = (s_qx0, s_qx1)[slot]
                        sc.dma_start(out=xsl[:, slot * ICH * 128:(slot + 1) * ICH * 128],
                                     in_=xlt[ds(t, 1)].squeeze(0)).then_inc(sxl, 16)

        def qs_weave(sc, st, t):
            with sc.If(t >= 2):
                tm, _ = s_assert_within(t - 2, 0, NT - 1)
                q_xload(sc, st, 0, tm, tm, "w")

        def q_scalar(sc, st, woven_og0=False):
            if woven_og0:
                with sc.Fori(NT - 2, NT) as t:
                    q_xload(sc, st, 0, t, t, "a")
                with sc.Fori(1, OG) as og:
                    with sc.Fori(0, NT) as t:
                        q_xload(sc, st, og, t, og * NT + t, "b")
            else:
                with sc.Fori(0, OG) as og:
                    with sc.Fori(0, NT) as t:
                        q_xload(sc, st, og, t, og * NT + t, "c")

        def q_group(te, st, og, oslot, t, n, tslot):
            sxl = (s_qx0, s_qx1)[tslot]
            te.wait_ge(sxl, 16 * (QX_B[st] + og * (NT // 2) + t // 2 + 1))
            with te.If(n >= 2):
                te.wait_ge(s_qev, QG_B[st] + n - 1)
            for mat in range(2):
                last = None
                for ich in range(ICH):
                    last = te.matmul(
                        bank(tslot * 2 + mat),
                        xsl[:, (tslot * ICH + ich) * 128:(tslot * ICH + ich) * 128 + 128],
                        wsl[:, (1 - oslot) * 2 * ICH * 512 + (mat * ICH + ich) * 512:
                               (1 - oslot) * 2 * ICH * 512 + (mat * ICH + ich) * 512 + 512],
                        start=(ich == 0), stop=(ich == ICH - 1))
                if mat == 1:
                    last.then_inc(s_qmm, 1)

        def q_weave(te, st, t):
            with te.If(t >= 2):
                tm, _ = s_assert_within(t - 2, 0, NT - 1)
                te.wait_ge(s_qw0, 16 * (QWS_B[st] + QWSLAB))
                with te.register(f"qw{st}") as rt:
                    te.reg_alu(rt, te.to_reg(t), 1, ALU.bitwise_and)
                    for tslot in (0, 1):
                        with te.If_eq(rt, tslot):
                            q_group(te, st, 0, 0, tm, tm, tslot)

        def q_t_loop(te, st, og, oslot, t0=0, tag=""):
            swl = (s_qw0, s_qw1)[oslot]
            te.wait_ge(swl, 16 * (QWS_B[st] + (og // 2 + 1) * QWSLAB))
            with te.Fori(t0, NT) as t:
                n = og * NT + t
                with te.register(f"qtt{st}{oslot}{tag}") as rt:
                    te.reg_alu(rt, te.to_reg(t), 1, ALU.bitwise_and)
                    for tslot in (0, 1):
                        with te.If_eq(rt, tslot):
                            q_group(te, st, og, oslot, t, n, tslot)

        def q_tensor(te, st, woven_og0=False):
            if woven_og0:
                q_t_loop(te, st, 0, 0, t0=NT - 2, tag="t")
                with te.Fori(1, OG) as og:
                    with te.register(f"qto{st}") as ro:
                        te.reg_alu(ro, te.to_reg(og), 1, ALU.bitwise_and)
                        with te.If_eq(ro, 0):
                            q_t_loop(te, st, og, 0)
                        with te.Else():
                            q_t_loop(te, st, og, 1)
            else:
                with te.Fori(0, OG) as og:
                    with te.register(f"qto{st}") as ro:
                        te.reg_alu(ro, te.to_reg(og), 1, ALU.bitwise_and)
                        with te.If_eq(ro, 0):
                            q_t_loop(te, st, og, 0)
                        with te.Else():
                            q_t_loop(te, st, og, 1)

        def q_ev(v, st, og, t, n, tag, ogslot=None):
            bq = (QG_B[st] + n) * 4
            v.wait_ge(s_qmm, QG_B[st] + n + 1)
            v.wait_ge(s_qbs, 16 * (QB_B[st] + og + 1))
            with v.If(n >= 2):
                v.wait_ge(s_qac, 16 * (QG_B[st] + n))
            with v.register(f"qv{st}{tag}") as r:
                v.reg_alu(r, v.to_reg(t), 1, ALU.bitwise_and)
                for slot in (0, 1):
                    with v.If_eq(r, slot):
                        yl = ylb[:, slot * 512:(slot + 1) * 512]
                        yd = ydb[:, slot * 512:(slot + 1) * 512]
                        yg = ystg[:, slot * 512:(slot + 1) * 512]
                        v.tensor_copy(yl, bank(slot * 2)).then_inc(s_qdv, 1)
                        v.wait_ge(s_qdv, bq + 1)
                        v.tensor_tensor(yd, bank(slot * 2 + 1), yl, ALU.subtract).then_inc(s_qdv, 1)
                        v.wait_ge(s_qdv, bq + 2)
                        v.scalar_tensor_tensor(yd, yd, maskb[:, ds(t, 1)], yl, ALU.mult, ALU.add).then_inc(s_qdv, 1)
                        v.wait_ge(s_qdv, bq + 3)
                        if ogslot is not None:
                            v.scalar_tensor_tensor(yg, yd, 0.5, bias05v[:, ogslot * 512:(ogslot + 1) * 512],
                                                   ALU.mult, ALU.add).then_inc(s_qdv, 1)
                        else:
                            with v.register(f"qo{st}{tag}{slot}") as ro:
                                v.reg_alu(ro, v.to_reg(og), 1, ALU.bitwise_and)
                                for os_ in (0, 1):
                                    with v.If_eq(ro, os_):
                                        v.scalar_tensor_tensor(yg, yd, 0.5, bias05v[:, os_ * 512:(os_ + 1) * 512],
                                                               ALU.mult, ALU.add).then_inc(s_qdv, 1)
                        v.wait_ge(s_qdv, bq + 4)
            v.engine_nop().then_inc(s_qev, 1)

        def qv_weave(v, st, t):
            with v.If(t >= 2):
                tm, _ = s_assert_within(t - 2, 0, NT - 1)
                q_ev(v, st, 0, tm, tm, "w", ogslot=0)

        def q_vector(v, st, woven_og0=False):
            if woven_og0:
                with v.Fori(NT - 2, NT) as t:
                    q_ev(v, st, 0, t, t, "a", ogslot=0)
                with v.Fori(1, OG) as og:
                    with v.Fori(0, NT) as t:
                        q_ev(v, st, og, t, og * NT + t, "b")
            else:
                with v.Fori(0, OG) as og:
                    with v.Fori(0, NT) as t:
                        q_ev(v, st, og, t, og * NT + t, "c")

        def q_acc(g, st, og, t, n, tag):
            g.wait_ge(s_qev, QG_B[st] + n + 1)
            with g.register(f"qg{st}{tag}") as r:
                g.reg_alu(r, g.to_reg(t), 1, ALU.bitwise_and)
                for slot in (0, 1):
                    with g.If_eq(r, slot):
                        g.dma_start(out=xs[ds(t * 128, 128), ts(og, 512)],
                                    in_=ystg[:, slot * 512:(slot + 1) * 512],
                                    accum_op=ALU.add).then_inc(s_qac, 16)

        def qg_weave(g, st, t):
            if NT <= 4:
                return
            with g.If(t >= 4):
                tm, _ = s_assert_within(t - 4, 0, NT - 1)
                q_acc(g, st, 0, tm, tm, "w")

        def q_gpsimd(g, st, woven_og0=False):
            if woven_og0:
                with g.Fori(NT - 4, NT) as t:
                    q_acc(g, st, 0, t, t, "a")
                with g.Fori(1, OG) as og:
                    with g.Fori(0, NT) as t:
                        q_acc(g, st, og, t, og * NT + t, "b")
            else:
                with g.Fori(0, OG) as og:
                    with g.Fori(0, NT) as t:
                        q_acc(g, st, og, t, og * NT + t, "c")
            g.wait_ge(s_qac, 16 * (QG_B[st] + NTOT))

        def a_sync(sy):
            with sy.Fori(0, NSUP) as sp:
                for mat in range(2):
                    nld = sp * 2 + mat
                    with sy.If(nld >= 1):
                        sy.wait_ge(s_amm, (sp * 2 + mat) * ICH)
                    sy.dma_start(out=ugslab, in_=upg_b[mat:mat + 1].squeeze(0)).then_inc(s_ug0, 16)
                sy.wait_ge(s_lc, LC_B[1] + NT * lkd(1))
                for m in range(1 + G):
                    with sy.Fori(0, OG) as og:
                        nf = sp * (1 + G) * OG + m * OG + og
                        with sy.If(nf >= 2):
                            sy.wait_ge(s_apm, (nf - 1) * SUP)
                        with sy.register(f"af{m}") as r:
                            sy.reg_alu(r, sy.to_reg(og), 1, ALU.bitwise_and)
                            for slot in (0, 1):
                                with sy.If_eq(r, slot):
                                    saf = (s_af0, s_af1)[slot]
                                    sy.dma_start(out=fsl[:, slot * UCH * 512:(slot + 1) * UCH * 512],
                                                 in_=fuse_b[m:m + 1].squeeze(0)[ds(og, 1)].squeeze(0)).then_inc(saf, 16)

        def a_gpsimd(g):
            with g.Fori(0, NSUP) as sp:
                for m in range(1 + G):
                    with g.Fori(0, OG) as og:
                        nf = sp * (1 + G) * OG + m * OG + og
                        g.wait_ge(s_apy, (nf + 1) * SUP)
                        with g.register(f"ac{m}") as r:
                            g.reg_alu(r, g.to_reg(og), 1, ALU.bitwise_and)
                            for slot in (0, 1):
                                with g.If_eq(r, slot):
                                    g.dma_start(out=xs[ds(sp * SUP * 128, SUP * 128), ts(og, 512)].rearrange("(jj p) o -> p jj o", p=128),
                                                in_=astg2[slot].rearrange("p (jj o) -> p jj o", jj=SUP),
                                                accum_op=ALU.add).then_inc((s_aac0, s_aac1)[slot], 16)
            NFTOT = NSUP * (1 + G) * OG
            g.wait_ge(s_aac0, 16 * ((NFTOT + 1) // 2))
            g.wait_ge(s_aac1, 16 * (NFTOT // 2))

        def a_scalar(sc):
            with sc.Fori(0, NSUP) as sp:
                with sc.If(sp >= 1):
                    sc.wait_ge(s_amm, sp * AMM_S)
                with sc.Fori(0, SUP) as jj:
                    t = sp * SUP + jj
                    sc.dma_start(out=axsl.rearrange("p (jj q) -> p jj q", jj=SUP)[:, ds(jj, 1)].squeeze(1),
                                 in_=xlt[ds(t, 1)].squeeze(0)).then_inc(s_ax, 16)
                sc.wait_ge(s_amm, sp * AMM_S + ICH)
                with sc.If(sp >= 1):
                    sc.wait_ge(s_adv, sp * ADV_S)
                    sc.wait_ge(s_atp, sp * (1 + G) * NRND_A)
                last = None
                for jj in range(SUP):
                    for ug in range(UG):
                        last = sc.activation(ub[:, (jj * UG + ug) * 512:(jj * UG + ug) * 512 + 512],
                                             bank(jj * UG + ug), AF.Copy)
                last.then_inc(s_aev, SUP * UG)
                for m in range(1 + G):
                    with sc.Fori(0, OG) as og:
                        nf = sp * (1 + G) * OG + m * OG + og
                        with sc.register(f"as{m}") as r:
                            sc.reg_alu(r, sc.to_reg(og), 1, ALU.bitwise_and)
                            for oslot in (0, 1):
                                with sc.If_eq(r, oslot):
                                    with sc.If(nf >= 2):
                                        sc.wait_ge((s_aac0, s_aac1)[oslot], 16 * (nf // 2))
                                    for jj in range(SUP):
                                        nap = nf * SUP + jj
                                        sc.wait_ge(s_apm, nap + 1)
                                        sc.activation(astg2[oslot][:, jj * 512:(jj + 1) * 512], bank(2 + (SUP * oslot + jj) % 4), AF.Copy).then_inc(s_apy, 1)

        def a_tensor(te):
            with te.Fori(0, NSUP) as sp:
                te.wait_ge(s_ax, 16 * (sp + 1) * SUP)
                for mat in range(2):
                    te.wait_ge(s_ug0, 16 * (sp * 2 + mat + 1))
                    for ich in range(ICH):
                        if ich == 0:
                            if mat == 0:
                                with te.If(sp >= 1):
                                    te.wait_ge(s_aev, sp * AEV_S)
                                    te.wait_ge(s_apy, sp * (1 + G) * OG * SUP)
                            else:
                                te.wait_ge(s_aev, sp * AEV_S + SUP * UG)
                        last = None
                        for jj in range(SUP):
                            for ug in range(UG):
                                last = te.matmul(bank(jj * UG + ug),
                                                 axsl[:, (jj * ICH + ich) * 128:(jj * ICH + ich) * 128 + 128],
                                                 ugslab[:, ich * U + ug * 512:ich * U + ug * 512 + 512],
                                                 start=(ich == 0), stop=(ich == ICH - 1))
                        last.then_inc(s_amm, 1)
                for m in range(1 + G):
                    src = ub if m == 0 else tmpm
                    if m == 0:
                        te.wait_ge(s_aev, (sp + 1) * AEV_S)
                    else:
                        te.wait_ge(s_adv, sp * ADV_S + ADV_END[m])
                    ngrp = sp * (1 + G) + m
                    for rnd in range(NRND_A):
                        ntile = min(TRND, NTT - rnd * TRND)
                        te.wait_ge(s_atv, ngrp * NRND_A + rnd)
                        last = None
                        for k in range(ntile):
                            kk = rnd * TRND + k
                            last = te.transpose(ptr[:, k * 128:(k + 1) * 128], src[:, kk * 128:(kk + 1) * 128], ident[:, :])
                        last.then_inc(s_atp, 1)
                    te.wait_ge(s_atv, (ngrp + 1) * NRND_A)
                    with te.Fori(0, OG) as og:
                        nf = sp * (1 + G) * OG + m * OG + og
                        with te.register(f"am{m}") as r:
                            te.reg_alu(r, te.to_reg(og), 1, ALU.bitwise_and)
                            for oslot in (0, 1):
                                with te.If_eq(r, oslot):
                                    saf = (s_af0, s_af1)[oslot]
                                    te.wait_ge(saf, 16 * (nf // 2 + 1))
                                    for jj in range(SUP):
                                        nap = nf * SUP + jj
                                        with te.If(nap >= 4):
                                            te.wait_ge(s_apy, nap - 3)
                                        last = None
                                        for uch in range(UCH):
                                            last = te.matmul(bank(2 + (SUP * oslot + jj) % 4),
                                                             tmpT[:, (jj * UCH + uch) * 128:(jj * UCH + uch) * 128 + 128],
                                                             fsl[:, (oslot * UCH + uch) * 512:(oslot * UCH + uch) * 512 + 512],
                                                             start=(uch == 0), stop=(uch == UCH - 1))
                                        last.then_inc(s_apm, 1)

        def a_vector(v):
            with v.Fori(0, NSUP) as sp:
                bD = sp * ADV_S
                j = [0]

                def step(inst):
                    inst.then_inc(s_adv, 1)
                    j[0] += 1

                def wj():
                    v.wait_ge(s_adv, bD + j[0])

                v.wait_ge(s_amm, (sp + 1) * AMM_S)
                for jj in range(SUP):
                    for ug in range(UG):
                        step(v.tensor_tensor(deltab[:, (jj * UG + ug) * 512:(jj * UG + ug) * 512 + 512],
                                             bank(jj * UG + ug), lpb[:, ug * 512:(ug + 1) * 512], ALU.subtract))
                v.engine_nop().then_inc(s_aev, SUP * UG)
                wj()
                for jj in range(SUP):
                    du = deltab[:, jj * U:(jj + 1) * U]
                    step(v.tensor_scalar(abst, du, -1.0, None, ALU.mult))
                    wj()
                    step(v.tensor_tensor(abst, abst, du, ALU.max))
                    wj()
                    step(v.tensor_scalar(keepb[:, jj * U:(jj + 1) * U], abst, 2.5, None, ALU.is_le))
                    wj()
                def evac_m(m):
                    ngrp = sp * (1 + G) + m
                    v.wait_ge(s_apm, ngrp * OG * SUP)
                    for rnd in range(NRND_A):
                        ntile = min(TRND, NTT - rnd * TRND)
                        v.wait_ge(s_atp, ngrp * NRND_A + rnd + 1)
                        last = None
                        for k in range(ntile):
                            kk = rnd * TRND + k
                            last = v.tensor_copy(tmpT[:, kk * 128:(kk + 1) * 128], ptr[:, k * 128:(k + 1) * 128])
                        last.then_inc(s_atv, 1)

                for m in range(1, 1 + G):
                    if m >= 2:
                        v.wait_ge(s_atp, (sp * (1 + G) + m) * NRND_A)
                    for jj in range(SUP):
                        du = deltab[:, jj * U:(jj + 1) * U]
                        pu = pband[:, jj * U:(jj + 1) * U]
                        tu = tmpm[:, jj * U:(jj + 1) * U]
                        uu = ub[:, jj * U:(jj + 1) * U]
                        ku = keepb[:, jj * U:(jj + 1) * U]
                        if m >= 2:
                            step(v.tensor_tensor(pu, pu if m > 2 else du, du, ALU.mult))
                            wj()
                        psrc = du if m == 1 else pu
                        step(v.tensor_scalar(tu, psrc, expd[:, m - 1:m], None, ALU.mult))
                        wj()
                        step(v.tensor_tensor(tu, tu, uu, ALU.mult))
                        wj()
                        if m > 4:
                            step(v.tensor_tensor(tu, tu, ku, ALU.mult))
                            wj()
                    evac_m(m - 1)
                evac_m(G)
                assert j[0] == ADV_S, (j[0], ADV_S)

        def bi_sync(sy):
            with sy.Fori(0, NFG) as fg:
                with sy.Fori(0, NT) as t:
                    n = fg * NT + t
                    with sy.If(n >= 2):
                        sy.wait_ge(s_bmm, 2 * (n - 1))
                    with sy.register("bsy") as r:
                        sy.reg_alu(r, sy.to_reg(t), 1, ALU.bitwise_and)
                        for slot in (0, 1):
                            with sy.If_eq(r, slot):
                                sbx = (s_bx0, s_bx1)[slot]
                                sy.dma_start(out=bxsl[:, slot * ICH * 128:(slot + 1) * ICH * 128],
                                             in_=xlt[ds(t, 1)].squeeze(0)).then_inc(sbx, 16)

        def bi_scalar2(sc):
            sc.preload_act_table(AF.Silu)
            with sc.Fori(0, NFG) as fg:
                with sc.Fori(0, NT) as t:
                    n = fg * NT + t
                    sc.wait_ge(s_bmm, 2 * n + 1)
                    with sc.If(n >= 2):
                        sc.wait_ge(s_btp, n - 1)
                    with sc.register("bsc2") as r:
                        sc.reg_alu(r, sc.to_reg(t), 1, ALU.bitwise_and)
                        for slot in (0, 1):
                            with sc.If_eq(r, slot):
                                sc.activation(sgb[:, slot * FGS:(slot + 1) * FGS], bank(slot * 2)[:, 0:FGS], AF.Silu).then_inc(s_bsg, 1)
                    sc.wait_ge(s_btv, n + 1)
                    sc.dma_start(out=hts[ds(t, 1)].squeeze(0)[:, ds(fg * FPG * 128, FPG * 128)],
                                 in_=hstg[:, :]).then_inc(s_bhw, 16)
            sc.wait_ge(s_bhw, 16 * NFG * NT)

        def bi_tensor(te):
            with te.Fori(0, NFG) as fg:
                with te.register("btf") as rf:
                    te.reg_alu(rf, te.to_reg(fg), 1, ALU.bitwise_and)
                    with te.Fori(0, NT) as t:
                        n = fg * NT + t
                        with te.register("btt") as rt:
                            te.reg_alu(rt, te.to_reg(t), 1, ALU.bitwise_and)
                            for fslot in (0, 1):
                                with te.If_eq(rf, fslot):
                                    sgu = (s_gu0, s_gu1)[fslot]
                                    te.wait_ge(sgu, 32 * (fg // 2 + 1))
                                    for tslot in (0, 1):
                                        with te.If_eq(rt, tslot):
                                            sbx = (s_bx0, s_bx1)[tslot]
                                            te.wait_ge(sbx, 16 * (fg * (NT // 2) + t // 2 + 1))
                                            with te.If(n >= 2):
                                                te.wait_ge(s_bdv, n - 1)
                                            for mat in range(2):
                                                last = None
                                                for ich in range(ICH):
                                                    last = te.matmul(
                                                        bank(tslot * 2 + mat)[:, 0:FGS],
                                                        bxsl[:, (tslot * ICH + ich) * 128:(tslot * ICH + ich) * 128 + 128],
                                                        gusl[:, ((fslot * 2 + mat) * ICH + ich) * FGS:((fslot * 2 + mat) * ICH + ich) * FGS + FGS],
                                                        start=(ich == 0), stop=(ich == ICH - 1))
                                                last.then_inc(s_bmm, 1)
                        te.wait_ge(s_bdv, n + 1)
                        with te.If(n >= 1):
                            te.wait_ge(s_btv, n)
                        with te.register("btt2") as rt2:
                            te.reg_alu(rt2, te.to_reg(t), 1, ALU.bitwise_and)
                            for tslot in (0, 1):
                                with te.If_eq(rt2, tslot):
                                    last = None
                                    for k in range(FPG):
                                        last = te.transpose(ptr[:, k * 128:(k + 1) * 128],
                                                            sgb[:, tslot * FGS + k * 128:tslot * FGS + (k + 1) * 128],
                                                            ident[:, :])
                                    last.then_inc(s_btp, 1)

        def bi_vector(v):
            with v.Fori(0, NFG) as fg:
                with v.Fori(0, NT) as t:
                    n = fg * NT + t
                    v.wait_ge(s_bsg, n + 1)
                    v.wait_ge(s_bmm, 2 * n + 2)
                    with v.register("bvt") as rt:
                        v.reg_alu(rt, v.to_reg(t), 1, ALU.bitwise_and)
                        for tslot in (0, 1):
                            with v.If_eq(rt, tslot):
                                v.tensor_tensor(sgb[:, tslot * FGS:(tslot + 1) * FGS],
                                                sgb[:, tslot * FGS:(tslot + 1) * FGS],
                                                bank(tslot * 2 + 1)[:, 0:FGS], ALU.mult).then_inc(s_bdv, 1)
                    v.wait_ge(s_btp, n + 1)
                    with v.If(n >= 1):
                        v.wait_ge(s_bhw, 16 * n)
                    last = None
                    for k in range(FPG):
                        last = v.tensor_copy(hstg[:, k * 128:(k + 1) * 128], ptr[:, k * 128:(k + 1) * 128])
                    last.then_inc(s_btv, 1)

        def bi_gpsimd(g):
            with g.Fori(0, NFG) as fg:
                with g.If(fg >= 2):
                    g.wait_ge(s_bmm, 2 * (fg - 1) * NT)
                with g.register("gfg") as r:
                    g.reg_alu(r, g.to_reg(fg), 1, ALU.bitwise_and)
                    for slot in (0, 1):
                        with g.If_eq(r, slot):
                            sgu = (s_gu0, s_gu1)[slot]
                            for mat in range(2):
                                g.dma_start(out=gusl[:, (slot * 2 + mat) * ICH * FGS:(slot * 2 + mat + 1) * ICH * FGS],
                                            in_=gu[mat:mat + 1].squeeze(0)[ds(fg, 1)].squeeze(0).rearrange("p i f -> p (i f)")).then_inc(sgu, 16)

        DQB = [0, FCH // 3, 2 * FCH // 3, FCH]

        def d_slabq(eng, og, qi):
            f0, f1 = DQB[qi], DQB[qi + 1]
            with eng.If(og >= 1):
                eng.wait_ge(s_dmm, og * NT)
            eng.dma_start(out=dsl[:, f0 * 512:f1 * 512],
                          in_=dwt_b[ds(og, 1)].squeeze(0)[:, f0 * 512:f1 * 512]).then_inc((s_dl, s_dl2, s_dl3)[qi], 16)

        def d_gpsimd(g):
            with g.Fori(0, OG) as og:
                d_slabq(g, og, 0)
                with g.Fori(0, NT) as t:
                    n = og * NT + t
                    g.wait_ge(s_dcp, n + 1)
                    with g.register("gdt") as r:
                        g.reg_alu(r, g.to_reg(t), 1, ALU.bitwise_and)
                        for slot in (0, 1):
                            with g.If_eq(r, slot):
                                g.dma_start(out=xs[ds(t * 128, 128), ts(og, 512)],
                                            in_=dstg[:, slot * 512:(slot + 1) * 512],
                                            accum_op=ALU.add).then_inc(s_dac, 16)
            g.wait_ge(s_dac, 16 * OG * NT)

        def d_sync(sy):
            with sy.Fori(0, OG) as og:
                d_slabq(sy, og, 1)
                with sy.Fori(0, NT) as t:
                    n = og * NT + t
                    with sy.If(n >= 2):
                        sy.wait_ge(s_dmm, n - 1)
                    with sy.register("dsy") as r:
                        sy.reg_alu(r, sy.to_reg(t), 1, ALU.bitwise_and)
                        for slot in (0, 1):
                            with sy.If_eq(r, slot):
                                shx = (s_hx0, s_hx1)[slot]
                                sy.dma_start(out=hsl[:, slot * FCH * 128:(slot + 1) * FCH * 128],
                                             in_=hts[ds(t, 1)].squeeze(0)).then_inc(shx, 16)

        def d_scalar(sc):
            with sc.Fori(0, OG) as og:
                d_slabq(sc, og, 2)
                with sc.Fori(0, NT) as t:
                    n = og * NT + t
                    sc.wait_ge(s_dmm, n + 1)
                    with sc.If(n >= 2):
                        sc.wait_ge(s_dac, 16 * n)
                    with sc.register("dsc2") as r:
                        sc.reg_alu(r, sc.to_reg(t), 1, ALU.bitwise_and)
                        for slot in (0, 1):
                            with sc.If_eq(r, slot):
                                sc.activation(dstg[:, slot * 512:(slot + 1) * 512], bank(slot), AF.Copy).then_inc(s_dcp, 1)

        def d_tensor(te):
            with te.Fori(0, OG) as og:
                te.wait_ge(s_dl, 16 * (og + 1))
                te.wait_ge(s_dl2, 16 * (og + 1))
                te.wait_ge(s_dl3, 16 * (og + 1))
                with te.Fori(0, NT) as t:
                    n = og * NT + t
                    with te.register("btd") as rt:
                        te.reg_alu(rt, te.to_reg(t), 1, ALU.bitwise_and)
                        for tslot in (0, 1):
                            with te.If_eq(rt, tslot):
                                shx = (s_hx0, s_hx1)[tslot]
                                te.wait_ge(shx, 16 * (og * (NT // 2) + t // 2 + 1))
                                with te.If(n >= 2):
                                    te.wait_ge(s_dcp, n - 1)
                                last = None
                                for f in range(FCH):
                                    last = te.matmul(bank(tslot),
                                                     hsl[:, (tslot * FCH + f) * 128:(tslot * FCH + f) * 128 + 128],
                                                     dsl[:, f * 512:(f + 1) * 512],
                                                     start=(f == 0), stop=(f == FCH - 1))
                                last.then_inc(s_dmm, 1)

        def mk_qphase(si, st):
            return dict(
                gpsimd=lambda x: (ln_xload(x, si, wg=lambda g, t: qg_weave(g, st, t)),
                                  q_gpsimd(x, st, woven_og0=True)),
                vector=lambda x: (ln_v(x, si, wv=lambda v, t: qv_weave(v, st, t)),
                                  q_vector(x, st, woven_og0=True)),
                scalar=lambda x: (ln_s(x, si, ws=lambda c2, t: qs_weave(c2, st, t)),
                                  q_scalar(x, st, woven_og0=True)),
                tensor=lambda x: (ln_t(x, si, weave=lambda te, t: q_weave(te, st, t)),
                                  q_tensor(x, st, woven_og0=True)),
                sync=lambda x: q_sync(x, st))

        PHASES = [
            dict(gpsimd=p0_g, vector=p0_v, scalar=p0_s),
            dict(gpsimd=dq_g, vector=dq_v, tensor=dq_t, scalar=dq_s),
            mk_qphase(0, 0),
            dict(gpsimd=lambda x: (ln_xload(x, 1), a_gpsimd(x)),
                 vector=lambda x: (ln_v(x, 1), a_vector(x)),
                 scalar=lambda x: (ln_s(x, 1), a_scalar(x)),
                 tensor=lambda x: (ln_t(x, 1), a_tensor(x)),
                 sync=a_sync),
            dict(gpsimd=bi_gpsimd, scalar=bi_scalar2, tensor=bi_tensor, vector=bi_vector, sync=bi_sync),
            dict(gpsimd=d_gpsimd, scalar=d_scalar, tensor=d_tensor, sync=d_sync),
            mk_qphase(2, 1),
        ]

        def run_phases(eng, name):
            phs = PHASES if nphases is None else PHASES[:nphases]
            for i, ph in enumerate(phs):
                if i > 0:
                    eng.wait_ge(s_bar, NENG * i)
                fn = ph.get(name)
                if fn:
                    fn(eng)
                eng.nop().then_inc(s_bar, 1)

        @block.gpsimd
        def _(g):
            run_phases(g, "gpsimd")
            if dbg:
                g.wait_ge(s_bar, NENG * len(PHASES if nphases is None else PHASES[:nphases]))
                g.dma_start(out=wdbg, in_=wsl).then_inc(s_p0, 16)
                g.dma_start(out=xdbg, in_=xsl).then_inc(s_p0, 16)
                g.wait_ge(s_p0, 10 * 16)
            g.wait_ge(s_bar, NENG * len(PHASES if nphases is None else PHASES[:nphases]))
            g.dma_start(out=y_out, in_=xs).then_inc(s_out, 16)
            g.wait_ge(s_out, 16)

        @block.vector
        def _(v):
            run_phases(v, "vector")

        @block.scalar
        def _(sc):
            run_phases(sc, "scalar")

        @block.tensor
        def _(te):
            run_phases(te, "tensor")

        @block.sync
        def _(sy):
            run_phases(sy, "sync")
            sy.wait_ge(s_out, 16)

    return nc


def prep_inputs(c, hidden_states, up_w, gate_w, local_point, lao, fuse_w,
                discount, ffn_gate_w, ffn_up_w, ffn_down_w, qweight, l4, l8, bias):
    D, U, F, G = c.D, c.U, c.F, c.G
    T = c.T
    x = np.ascontiguousarray(hidden_states.reshape(-1, D))
    assert x.shape[0] == NCORES * T

    upg = np.stack([up_w.T, gate_w.T]).reshape(2, c.ICH, 128, U)
    upg = np.ascontiguousarray(upg.transpose(0, 2, 1, 3).reshape(2, 128, c.ICH * U).astype(np.float32))
    Wms = [lao] + [fuse_w[:, :, m] for m in range(G)]
    fz = np.empty((1 + G, c.OG, 128, c.UCH, 512), np.float32)
    for m, W in enumerate(Wms):
        wt = W.T.reshape(c.UCH, 128, c.OG, 512)
        fz[m] = wt.transpose(2, 1, 0, 3)
    gz = np.empty((2, c.NFG, 128, c.ICH, c.FGS), np.float32)
    for mat, W in enumerate([ffn_gate_w, ffn_up_w]):
        wt = W.T.reshape(c.ICH, 128, c.NFG, c.FGS)
        gz[mat] = wt.transpose(2, 1, 0, 3)
    dt_ = ffn_down_w.T.reshape(c.FCH, 128, c.OG, 512)
    dz = np.ascontiguousarray(dt_.transpose(2, 1, 0, 3))
    qz, l4z, l8z = [], [], []
    for k in range(NCORES):
        sl = slice(k * c.OSH, (k + 1) * c.OSH)
        qz.append(np.ascontiguousarray(
            qweight[:, sl, :].transpose(1, 0, 2).reshape(c.OT, 128, 8, c.QW)))
        l4z.append(np.ascontiguousarray(l4[sl].reshape(c.OT, 128, c.NCL)))
        l8z.append(np.ascontiguousarray(l8[sl].reshape(c.OT, 128, c.NCH)))
    identc = np.eye(128, dtype=np.float32).astype(bass_bf16())
    clsv = np.broadcast_to(np.arange(c.NCH, dtype=np.float32), (128, c.NCH)).copy()
    shv = np.broadcast_to((np.uint32(1) << np.arange(32, dtype=np.uint32)).view(np.int32), (128, 32)).copy()
    lpr = np.broadcast_to(local_point.astype(np.float32), (128, U)).copy()
    dscr = np.broadcast_to(discount.astype(np.float32), (128, G)).copy()
    bzr = np.broadcast_to(bias.astype(np.float32), (128, D)).copy()
    cnv = np.broadcast_to(np.array([1e-5, 1.0 / D, -1.0, 0.0], np.float32), (128, 4)).copy()

    maps = []
    for k in range(NCORES):
        maps.append(dict(
            x_in=np.ascontiguousarray(x[k * T:(k + 1) * T]),
            upgate=upg, fuse=fz, gu=gz, dwt=dz,
            qwt=qz[k], lut4=l4z[k], lut8=l8z[k],
            identc=identc, clsvec=clsv, shiftv=shv,
            lp_rep=lpr, disc_rep=dscr, bias_rep=bzr, constv=cnv,
        ))
    return maps


def bass_bf16():
    import ml_dtypes
    return ml_dtypes.bfloat16


def kernel(L0_attn_hidden_states, L0_attn_up_w, L0_attn_gate_w,
           L0_attn_local_point, L0_attn_local_approx_output, L0_attn_fuse_weight,
           L0_attn_discount_factor, L0_attn_ffn_gate_w, L0_attn_ffn_up_w,
           L0_attn_ffn_down_w, L0_ffn_qweight, L0_ffn_lut4, L0_ffn_lut8, L0_ffn_bias):
    c = REAL
    maps = prep_inputs(
        c,
        np.asarray(L0_attn_hidden_states, np.float32),
        np.asarray(L0_attn_up_w, np.float32),
        np.asarray(L0_attn_gate_w, np.float32),
        np.asarray(L0_attn_local_point, np.float32),
        np.asarray(L0_attn_local_approx_output, np.float32),
        np.asarray(L0_attn_fuse_weight, np.float32),
        np.asarray(L0_attn_discount_factor, np.float32),
        np.asarray(L0_attn_ffn_gate_w, np.float32),
        np.asarray(L0_attn_ffn_up_w, np.float32),
        np.asarray(L0_attn_ffn_down_w, np.float32),
        np.asarray(L0_ffn_qweight, np.int32),
        np.asarray(L0_ffn_lut4, np.float32),
        np.asarray(L0_ffn_lut8, np.float32),
        np.asarray(L0_ffn_bias, np.float32),
    )
    nc = build(c)
    nc.finalize()
    res = run_bass_kernel_spmd(nc, maps, core_ids=list(range(NCORES)))
    out = np.concatenate([r["y_out"] for r in res.results], axis=0)
    return out.reshape(4, 4096, 4096)
